# version 29
# baseline (speedup 1.0000x reference)
import numpy as np
import ml_dtypes

import concourse.bass as bass
import concourse.mybir as mybir
from concourse.bass_utils import run_bass_kernel_spmd

F32 = mybir.dt.float32
BF16 = mybir.dt.bfloat16
AF = mybir.ActivationFunctionType
ALU = mybir.AluOpType
AX = mybir.AxisListType.X

P = 128
S = 2048
D = 1024
DC = 8
TPS = 16
NSEQ = 2
E = 8
H = 512
HCN = 4
NBLK = 2
BT = 8
NCORE = 8

ENGINES = ("sync", "tensor", "vector", "scalar", "gpsimd")


class Ctr:
    __slots__ = ("sem", "n")

    def __init__(self, sem):
        self.sem = sem
        self.n = 0


class Prog:
    def __init__(self):
        self.ops = {e: [] for e in ENGINES}
        self.cum = set()

    def op(self, eng, meth, *args, waits=(), inc=None, amt=1, **kw):
        waits = [(c, c.n if c in self.cum else v) for (c, v) in waits]
        for (c, v) in waits:
            assert v <= c.n, f"wait targets future value {v} > {c.n} ({meth})"
        pair = None
        if inc is not None:
            inc.n += amt
            pair = (inc.sem, amt)
        wl = tuple((c.sem, int(v)) for c, v in waits if v > 0)

        def f(e, meth=meth, args=args, kw=kw, wl=wl, pair=pair):
            for sm, v in wl:
                e.wait_ge(sm, v)
            ins = getattr(e, meth)(*args, **kw)
            if pair is not None:
                ins.then_inc(*pair)

        self.ops[eng].append(f)
        return inc.n if inc is not None else None

    def wait(self, eng, *waits):
        for (c, v) in waits:
            assert v <= c.n
        wl = tuple((c.sem, int(v)) for c, v in waits if v > 0)

        def f(e, wl=wl):
            for sm, v in wl:
                e.wait_ge(sm, v)

        self.ops[eng].append(f)


def build_nc(debug=False):
    nc = bass.Bass("TRN2", target_bir_lowering=False)

    dr = {}
    dr["xin"] = nc.dram_tensor("xin", [NSEQ, S, D], F32, kind="ExternalInput").ap()
    dr["wqp"] = nc.dram_tensor("wqp", [D, D], F32, kind="ExternalInput").ap()
    dr["wkp"] = nc.dram_tensor("wkp", [D, D], F32, kind="ExternalInput").ap()
    dr["wvt"] = nc.dram_tensor("wvt", [D, D], F32, kind="ExternalInput").ap()
    dr["wot"] = nc.dram_tensor("wot", [D, D], F32, kind="ExternalInput").ap()
    dr["cosb"] = nc.dram_tensor("cosb", [4, P, S], F32, kind="ExternalInput").ap()
    dr["sinb"] = nc.dram_tensor("sinb", [4, P, S], F32, kind="ExternalInput").ap()
    dr["cosq"] = nc.dram_tensor("cosq", [4, P, S], F32, kind="ExternalInput").ap()
    dr["sinq"] = nc.dram_tensor("sinq", [4, P, S], F32, kind="ExternalInput").ap()
    dr["gwl"] = nc.dram_tensor("gwl", [P, DC * E], F32, kind="ExternalInput").ap()
    dr["gbb"] = nc.dram_tensor("gbb", [P, E], F32, kind="ExternalInput").ap()
    dr["w1l"] = nc.dram_tensor("w1l", [E, D, H], F32, kind="ExternalInput").ap()
    dr["b1l"] = nc.dram_tensor("b1l", [P, E * HCN], F32, kind="ExternalInput").ap()
    dr["w2l"] = nc.dram_tensor("w2l", [E, H, D], F32, kind="ExternalInput").ap()
    dr["b2i"] = nc.dram_tensor("b2i", [E, D], F32, kind="ExternalInput").ap()
    dr["idin"] = nc.dram_tensor("idin", [P, P], BF16, kind="ExternalInput").ap()
    dr["sel8"] = nc.dram_tensor("sel8", [E, E * P], BF16, kind="ExternalInput").ap()
    dr["yout"] = nc.dram_tensor("yout", [NSEQ, S, D], F32, kind="ExternalOutput").ap()
    dr["xres"] = nc.dram_tensor("xres", [NSEQ, S, D], F32, kind="Internal").ap()
    dr["w1b"] = nc.dram_tensor("w1b", [E, P, 4096], BF16, kind="Internal").ap()
    dr["w2b"] = nc.dram_tensor("w2b", [E, P, 4096], BF16, kind="Internal").ap()
    dr["qkb"] = nc.dram_tensor("qkb", [4, P, 8192], BF16, kind="Internal").ap()
    dr["cbx"] = nc.dram_tensor("cbx", [4, P, S], BF16, kind="Internal").ap()
    dr["sbx"] = nc.dram_tensor("sbx", [4, P, S], BF16, kind="Internal").ap()
    dr["cqx"] = nc.dram_tensor("cqx", [4, P, S], BF16, kind="Internal").ap()
    dr["sqx"] = nc.dram_tensor("sqx", [4, P, S], BF16, kind="Internal").ap()

    dbg = {}
    if debug:
        for nm, shp, dt in [
            ("dbg_lnfm", [P, DC * S], BF16),
            ("dbg_q", [P, DC * S], BF16),
            ("dbg_k", [P, DC * S], BF16),
            ("dbg_v", [P, TPS * D], BF16),
            ("dbg_lnb", [P, DC * 1024], BF16),
            ("dbg_gfm", [E, S], BF16),
            ("dbg_gh", [P, 16384], BF16),
        ]:
            dbg[nm] = nc.dram_tensor(nm, shp, dt, kind="Internal").ap()

    import contextlib

    stack = contextlib.ExitStack()
    sbf = lambda nm, shp, dt: stack.enter_context(nc.sbuf_tensor(nm, shp, dt))
    psf = lambda nm, shp, dt: stack.enter_context(nc.psum_tensor(nm, shp, dt))
    semf = lambda nm: Ctr(stack.enter_context(nc.semaphore(nm)))

    with stack:
        sbuf = dict(
            LNFM=sbf("LNFM", [P, 16384], BF16),
            QFM=sbf("QFM", [P, 16384], BF16),
            KFM=sbf("KFM", [P, 16384], BF16),
            VTM=sbf("VTM", [P, 16384], BF16),
            WBUF=sbf("WBUF", [P, 16384], BF16),
            STAGE=sbf("STAGE", [P, 2048], F32),
            TBL=sbf("TBL", [P, 4096], BF16),
            XI=sbf("XI", [P, 2048], F32),
            XO=sbf("XO", [P, 2048], F32),
            LNT=sbf("LNT", [P, 2048], BF16),
            AB=sbf("AB", [P, 2048], BF16),
            SV=sbf("SV", [P, 64], F32),
            SVG=sbf("SVG", [P, 384], F32),
            GTB=sbf("GTB", [P, 64], BF16),
            SVX=sbf("SVX", [P, 48], F32),
            SVY=sbf("SVY", [P, 44], F32),
            ONEC=sbf("ONEC", [P, 1], BF16),
            GWSB=sbf("GWSB", [P, DC * E], BF16),
            GBB=sbf("GBB", [P, E], F32),
            B1SB=sbf("B1SB", [P, E * HCN], F32),
            B2SB=sbf("B2SB", [E, D], BF16),
            SEL8=sbf("SEL8", [E, E * P], BF16),
            IDT=sbf("IDT", [P, P], BF16),
            PQ0a=psf("PQ0a", [P, 512], F32),
            PQ0b=psf("PQ0b", [P, 512], F32),
            PQ1a=psf("PQ1a", [P, 512], F32),
            PQ1b=psf("PQ1b", [P, 512], F32),
            PPo=psf("PPo", [P, 1024], F32),
            PPt=psf("PPt", [P, 1024], BF16),
            PPt2=psf("PPt2", [P, 1024], BF16),
        )
        cs = dict(cW=semf("cW"), cW1=semf("cW1"), cWD=semf("cWD"), cWD2=semf("cWD2"),
                  cWT=semf("cWT"), cWT2=semf("cWT2"), cPL=semf("cPL"), cPL2=semf("cPL2"),
                  cTB=semf("cTB"), cTB2=semf("cTB2"),
                  cX0=semf("cX0"), cX1=semf("cX1"),
                  cST0=semf("cST0"), cST1=semf("cST1"), cDB=semf("cDB"), cC=semf("cC"),
                  cT=semf("cT"), cV=semf("cV"), cS=semf("cS"), cG=semf("cG"))

        pr = Prog()
        pr.cum.add(cs["cDB"])
        build_body(pr, dbg, dr, sbuf, cs)

        with nc.Block() as block:
            for eng in ENGINES:
                fns = pr.ops[eng]
                getattr(block, eng)(lambda e, fns=fns: [f(e) for f in fns])

    return nc


def build_body(pr, dbg, dr, sbuf, cs):
    xin, yout, xres = dr["xin"], dr["yout"], dr["xres"]
    LNFM, QFM, KFM, VTM, WBUF = (sbuf[k] for k in ("LNFM", "QFM", "KFM", "VTM", "WBUF"))
    STAGE, TBL, XI, XO, LNT, AB = (sbuf[k] for k in ("STAGE", "TBL", "XI", "XO", "LNT", "AB"))
    SV, SVG, GTB, GWSB, GBB = (sbuf[k] for k in ("SV", "SVG", "GTB", "GWSB", "GBB"))
    B1SB, B2SB, SEL8, IDT = (sbuf[k] for k in ("B1SB", "B2SB", "SEL8", "IDT"))
    ONEC = sbuf["ONEC"]
    SVX = sbuf["SVX"]
    SVY = sbuf["SVY"]
    PQ0a, PQ0b, PQ1a, PQ1b, PPo, PPt, PPt2 = (
        sbuf[k] for k in ("PQ0a", "PQ0b", "PQ1a", "PQ1b", "PPo", "PPt", "PPt2"))
    PQBK = [PQ0a, PQ0b, PQ1a, PQ1b]
    cT, cV, cS, cG = (cs[k] for k in ("cT", "cV", "cS", "cG"))
    cWD = cs["cWD"]
    cWDs = [cs["cWD"], cs["cWD2"]]
    cWTs = [cs["cWT"], cs["cWT2"]]
    cPLs = [cs["cPL"], cs["cPL2"]]
    cTBs = [cs["cTB"], cs["cTB2"]]
    wd_i = [0]
    pl_i = [0]
    cWs = [cs["cW"], cs["cW1"]]
    cXs = [cs["cX0"], cs["cX1"]]
    cSTs = [cs["cST0"], cs["cST1"]]
    cDB = cs["cDB"]
    cC = cs["cC"]

    pr.op("sync", "dma_start", IDT[:], dr["idin"][:], inc=cC, amt=16)
    gww = pr.op("gpsimd", "dma_start", GWSB[:], dr["gwl"][:], waits=[(cWD, cWD.n)], inc=cWD, amt=16)
    pr.op("sync", "dma_start", GBB[:], dr["gbb"][:], inc=cC, amt=16)
    pr.op("sync", "dma_start", B1SB[:], dr["b1l"][:], inc=cC, amt=16)
    pr.op("sync", "dma_start", SEL8[:], dr["sel8"][:], inc=cC, amt=16)
    pr.op("vector", "memset", SV[:, 63:64], 1e-5, inc=cV)
    onec_rdy = pr.op("vector", "memset", ONEC[:], 1.0, inc=cV)
    b2w = pr.op("gpsimd", "dma_start", B2SB[:], dr["b2i"][:], waits=[(cWD, cWD.n)], inc=cWD, amt=16)
    b2c = (cWD, b2w)
    stage_free = [[], []]
    CONSTW = 64

    lnfm_free, qfm_free, kfm_free, vtm_free, ab_free = [], [], [], [], []
    wslot_free = [[], []]
    xi_free = [[], []]
    xo_free = [[], []]
    tbl_free = [[], []]
    lnt_free = [[], []]
    ppt_free = [[], []]
    pqf = [[], [], [], []]
    ppo_free = []
    sv_free = [[], []]
    svq_free = [[], []]
    svg_free = [[] for _ in range(8)]
    xres_st = [None] * (NSEQ * TPS)
    svx_free = [[], []]
    svx_r = [None] * TPS
    svy_free = [[], []]
    svy_rd = []
    svy_r = [None] * TPS
    gtb_free = []
    stg_i = [0]

    def load_weight_bf16(src3, dst, dst_col, pre=(), eng=None):
        C, ncols = src3.shape[1], src3.shape[2]
        dview = dst[:, dst_col:dst_col + C * ncols].rearrange(
            "p (c n) -> p c n", n=ncols)
        sem = cWDs[wd_i[0] % 2]
        wd_i[0] += 1
        w = pr.op("gpsimd", "dma_start", dview, src3[:, :, :],
                  waits=[(sem, sem.n)] + list(pre), inc=sem, amt=16)
        return (sem, w)

    def precast(dstap, srcap, chain=None):
        if chain is None:
            chain = wd_i[0] % 2
            wd_i[0] += 1
        sem = cWDs[chain]
        w = pr.op("gpsimd", "dma_start", dstap, srcap,
                  waits=[(sem, sem.n)], inc=sem, amt=16)
        return (sem, w)

    def load_plain(dst, dst_col, width, srcap, pre=(), eng="sync"):
        sem = cPLs[pl_i[0] % 2]
        pl_i[0] += 1
        w = pr.op(eng, "dma_start", dst[:, dst_col:dst_col + width], srcap,
                  waits=[(sem, sem.n)] + list(pre), inc=sem, amt=16)
        return (sem, w)

    def prefetch_ln1_stats(s, tiles=None, eng="sync"):
        nonlocal svy_rd
        for t in (range(TPS) if tiles is None else tiles):
            k = stg_i[0] % 2
            stg_i[0] += 1
            par = t % 2
            cb = 32 + 6 * par
            c_ss, c_s, c_b, c_t = cb, cb + 1, cb + 2, cb + 3
            w = pr.op(eng, "dma_start", STAGE[:, k * 1024:(k + 1) * 1024],
                      xin[s, t * P:(t + 1) * P, :],
                      waits=list(stage_free[k]), inc=cWs[k], amt=16)
            sm = pr.op("vector", "reduce_sum", SVY[:, c_s:c_s + 1],
                       STAGE[:, k * 1024:(k + 1) * 1024], AX,
                       waits=[(cWs[k], w)] + list(svy_free[par]) + svy_rd, inc=cV)
            sq = pr.op("scalar", "activation", STAGE[:, k * 1024:(k + 1) * 1024],
                       STAGE[:, k * 1024:(k + 1) * 1024], AF.Square,
                       waits=[(cV, sm)] + list(svy_free[par]) + svy_rd, inc=cS,
                       accum_out=SVY[:, c_ss:c_ss + 1])
            svy_rd = []
            stage_free[k] = [(cS, sq)]
            b_ = pr.op("vector", "tensor_scalar", SVY[:, c_b:c_b + 1],
                       SVY[:, c_ss:c_ss + 1], 1.0 / D, 1e-5, ALU.mult,
                       ALU.add, waits=[(cS, sq)], inc=cV)
            m_ = pr.op("scalar", "activation", SVY[:, t:t + 1],
                       SVY[:, c_s:c_s + 1], AF.Copy, scale=1.0 / D, inc=cS)
            mm_ = pr.op("scalar", "activation", SVY[:, c_t:c_t + 1],
                        SVY[:, c_s:c_s + 1], AF.Square, scale=1.0 / D, inc=cS)
            l_ = pr.op("scalar", "activation", SVY[:, c_t:c_t + 1],
                       SVY[:, c_t:c_t + 1], AF.Ln, scale=-1.0,
                       bias=SVY[:, c_b:c_b + 1],
                       waits=[(cS, mm_), (cV, b_)], inc=cS)
            r_ = pr.op("scalar", "activation", SVY[:, TPS + t:TPS + t + 1],
                       SVY[:, c_t:c_t + 1], AF.Exp, scale=-0.5,
                       waits=[(cS, l_)], inc=cS)
            svy_free[par] = [(cS, r_)]
            svy_r[t] = (cS, r_)

    def ln_tile(sl, evac_view, psl, extra_evac_waits):
        b0 = 16 + 8 * sl
        c_s, c_ss, c_b, c_m, c_mm, c_ln, c_r = range(b0, b0 + 7)
        xiw = xi_free[sl][0]
        sq = pr.op("scalar", "activation", XO[:, sl * 1024:(sl + 1) * 1024],
                   XI[:, sl * 1024:(sl + 1) * 1024], AF.Square,
                   waits=[xiw] + list(xo_free[sl]) + list(sv_free[sl]),
                   inc=cS, accum_out=SV[:, c_ss:c_ss + 1])
        xo_free[sl] = [(cS, sq)]
        sm = pr.op("vector", "reduce_sum", SV[:, c_s:c_s + 1],
                   XI[:, sl * 1024:(sl + 1) * 1024], AX,
                   waits=[xiw] + list(sv_free[sl]), inc=cV)
        bq = pr.op("vector", "tensor_scalar", SV[:, c_b:c_b + 1],
                   SV[:, c_ss:c_ss + 1], 1.0 / D, 1e-5, ALU.mult, ALU.add,
                   waits=[(cS, sq), (cV, sm)], inc=cV)
        m1_ = pr.op("scalar", "activation", SV[:, c_m:c_m + 1],
                    SV[:, c_s:c_s + 1], AF.Copy, scale=1.0 / D,
                    waits=[(cV, bq)], inc=cS)
        mm_ = pr.op("scalar", "activation", SV[:, c_mm:c_mm + 1],
                    SV[:, c_s:c_s + 1], AF.Square, scale=1.0 / D, inc=cS)
        l1 = pr.op("scalar", "activation", SV[:, c_ln:c_ln + 1],
                   SV[:, c_mm:c_mm + 1], AF.Ln, bias=SV[:, c_b:c_b + 1],
                   scale=-1.0, waits=[(cS, mm_)], inc=cS)
        r1 = pr.op("scalar", "activation", SV[:, c_r:c_r + 1],
                   SV[:, c_ln:c_ln + 1], AF.Exp, scale=-0.5,
                   waits=[(cS, l1)], inc=cS)
        sv_free[sl] = [(cS, r1)]
        ap = pr.op("vector", "tensor_scalar", LNT[:, sl * 1024:(sl + 1) * 1024],
                   XI[:, sl * 1024:(sl + 1) * 1024], SV[:, c_m:c_m + 1],
                   SV[:, c_r:c_r + 1], ALU.subtract, ALU.mult,
                   waits=[(cS, r1)] + list(lnt_free[sl]), inc=cV)
        xi_free[sl] = [(cV, ap)]
        pi = 0 if psl is PPt else 1
        tp = None
        for dc_ in range(DC):
            tp = pr.op("tensor", "transpose", psl[:, dc_ * P:(dc_ + 1) * P],
                       LNT[:, sl * 1024 + dc_ * P: sl * 1024 + (dc_ + 1) * P],
                       IDT[:],
                       waits=([(cV, ap), (cC, CONSTW)] + list(ppt_free[pi])) if dc_ == 0 else [],
                       inc=cT if dc_ == DC - 1 else None)
        lnt_free[sl] = [(cT, tp)]
        return (psl, pi, tp, evac_view, list(extra_evac_waits))

    def ln_evac(info):
        psl, pi, tp, evac_view, extra = info
        pv = psl[:, :].rearrange("p (c w) -> p c w", c=DC)
        ev = pr.op("scalar", "copy", evac_view, pv[:, :, :],
                   waits=[(cT, tp)] + extra, inc=cS)
        ppt_free[pi] = [(cS, ev)]
        return ev

    pc = {}
    for s in range(NSEQ):
        if s == 0:
            prefetch_ln1_stats(0, tiles=range(0, 4))
        wv_rdy = load_weight_bf16(dr["wvt"].rearrange("(c p) n -> p c n", p=P),
                                  WBUF, 0, pre=wslot_free[0])
        wk_rdy = load_weight_bf16(dr["wkp"].rearrange("(c p) n -> p c n", p=P),
                                  WBUF, 8192, pre=wslot_free[1])
        wslot_free[0] = []
        wslot_free[1] = []
        if s == 0:
            prefetch_ln1_stats(0, tiles=range(4, TPS))

        lv = LNFM[:, :].rearrange("p (c s2) -> p c s2", c=DC)
        ln1_ev = [0] * TPS
        v_done = 0
        pend = None

        def emit_v(t, ev):
            nonlocal v_done, vtm_free, ppo_free
            use_pp0 = (t % 2 == 1) and t <= 8
            fr = (pqf[0] + pqf[1]) if use_pp0 else ppo_free
            mm = None
            for dc_ in range(DC):
                for n2 in range(2):
                    pvs = PQBK[n2][:, 0:512] if use_pp0 else PPo[:, n2 * 512:(n2 + 1) * 512]
                    mm = pr.op(
                        "tensor", "matmul", pvs,
                        LNFM[:, dc_ * 2048 + t * P: dc_ * 2048 + (t + 1) * P],
                        WBUF[:, dc_ * 1024 + n2 * 512: dc_ * 1024 + (n2 + 1) * 512],
                        start=(dc_ == 0), stop=(dc_ == DC - 1),
                        waits=([(cS, ev), wv_rdy] + list(fr)) if (dc_ == 0 and n2 == 0) else [],
                        inc=cT if (dc_ == DC - 1 and n2 == 1) else None)
            if use_pp0:
                pr.op("scalar", "copy", VTM[:, t * 1024:t * 1024 + 512],
                      PQ0a[:, 0:512],
                      waits=[(cT, mm)] + (list(vtm_free) if t == 0 else []),
                      inc=cS)
                v_done = pr.op("scalar", "copy",
                               VTM[:, t * 1024 + 512:(t + 1) * 1024],
                               PQ0b[:, 0:512], inc=cS)
            else:
                v_done = pr.op("scalar", "copy", VTM[:, t * 1024:(t + 1) * 1024],
                               PPo[:, :],
                               waits=[(cT, mm)] + (list(vtm_free) if t == 0 else []),
                               inc=cS)
            if t == 0:
                vtm_free = []
            if use_pp0:
                pqf[0] = [(cS, v_done)]
                pqf[1] = [(cS, v_done)]
            else:
                ppo_free = [(cS, v_done)]

        def emit_pair(FM, base_w, wrdy, c, h, tbl_evac=False):
            nonlocal ab_free
            sl = 0 if tbl_evac else (c * 2 + h) % 2
            if s == 0:
                cosrc = dr["cosb"] if FM is KFM else dr["cosq"]
                sisrc = dr["sinb"] if FM is KFM else dr["sinq"]
                teng = "gpsimd"
                tch = cWTs
                pcw = []
            else:
                cosrc = dr["cbx"] if FM is KFM else dr["cqx"]
                sisrc = dr["sbx"] if FM is KFM else dr["sqx"]
                teng = "sync"
                tch = cTBs
                pcw = [pc["tb"]]
            tcc_s = pr.op(teng, "dma_start", TBL[:, sl * 2048: sl * 2048 + 1024],
                          cosrc[c, :, h * 1024:(h + 1) * 1024],
                          waits=[(tch[0], tch[0].n)] + list(tbl_free[sl]) + pcw,
                          inc=tch[0], amt=16)
            tcc = (tch[0], tcc_s)
            tcs_s = pr.op(teng, "dma_start",
                          TBL[:, sl * 2048 + 1024: sl * 2048 + 2048],
                          sisrc[c, :, h * 1024:(h + 1) * 1024],
                          waits=[(tch[1], tch[1].n)] + list(tbl_free[sl]) + pcw,
                          inc=tch[1], amt=16)
            tcs = (tch[1], tcs_s)
            cosl = TBL[:, sl * 2048: sl * 2048 + 1024]
            sinl = TBL[:, sl * 2048 + 1024: sl * 2048 + 2048]
            if tbl_evac:
                eq0, eq1 = TBL[:, 2048:3072], TBL[:, 3072:4096]
                eqf = tbl_free[1]
            else:
                eq0, eq1 = LNT[:, 0:1024], LNT[:, 1024:2048]

            mgd = [None] * 4
            for pi, part in enumerate((c, c + 4)):
                for dc_ in range(DC):
                    for n2 in range(2):
                        first = dc_ == 0 and n2 == 0
                        mm = pr.op(
                            "tensor", "matmul",
                            PQBK[pi * 2 + n2][:, 0:512],
                            WBUF[:, base_w + dc_ * 1024 + part * P: base_w + dc_ * 1024 + (part + 1) * P],
                            LNFM[:, dc_ * 2048 + h * 1024 + n2 * 512: dc_ * 2048 + h * 1024 + (n2 + 1) * 512],
                            start=(dc_ == 0), stop=(dc_ == DC - 1),
                            waits=((list(wrdy) + [(cS, ln1_ev[8 * h + 7])] if (first and pi == 0) else []) + (pqf[pi * 2 + n2] if dc_ == 0 else [])),
                            inc=cT if dc_ == DC - 1 else None)
                        if dc_ == DC - 1:
                            mgd[pi * 2 + n2] = mm
            e1 = pr.op("scalar", "copy", eq0[:, 0:512], PQ0a[:, 0:512],
                       waits=[(cT, mgd[0])] + (list(eqf) if tbl_evac else list(lnt_free[0])), inc=cS)
            e1b = pr.op("scalar", "copy", eq0[:, 512:1024], PQ0b[:, 0:512],
                        waits=[(cT, mgd[1])], inc=cS)
            e2a = pr.op("scalar", "copy", eq1[:, 0:512], PQ1a[:, 0:512],
                        waits=[(cT, mgd[2])] + ([] if tbl_evac else list(lnt_free[1])), inc=cS)
            e2 = pr.op("scalar", "copy", eq1[:, 512:1024], PQ1b[:, 0:512],
                       waits=[(cT, mgd[3])], inc=cS)
            pqf[0] = [(cS, e1)]
            pqf[1] = [(cS, e1b)]
            pqf[2] = [(cS, e2a)]
            pqf[3] = [(cS, e2)]
            qr = eq0
            qi = eq1
            oc = FM[:, c * 2048 + h * 1024: c * 2048 + (h + 1) * 1024]
            oc4 = FM[:, (c + 4) * 2048 + h * 1024: (c + 4) * 2048 + (h + 1) * 1024]
            t1 = pr.op("vector", "tensor_tensor", oc, qr, cosl, ALU.mult,
                       waits=[(cS, e2), tcc, tcs] + list(ab_free), inc=cV)
            t2 = pr.op("vector", "tensor_tensor", AB[:, 0:1024], qi,
                       sinl, ALU.mult, inc=cV)
            t3 = pr.op("vector", "tensor_tensor", oc, oc, AB[:, 0:1024],
                       ALU.subtract, waits=[(cV, t2)], inc=cV)
            t4 = pr.op("vector", "tensor_tensor", oc4, qr, sinl,
                       ALU.mult, inc=cV)
            t5 = pr.op("vector", "tensor_tensor", AB[:, 1024:2048], qi,
                       cosl, ALU.mult, inc=cV)
            t6 = pr.op("vector", "tensor_tensor", oc4, oc4,
                       AB[:, 1024:2048], ALU.add, waits=[(cV, t5)],
                       inc=cV)
            if tbl_evac:
                tbl_free[1] = [(cV, t6)]
            else:
                lnt_free[0] = [(cV, t6)]
                lnt_free[1] = [(cV, t6)]
            ab_free = [(cV, t6)]
            tbl_free[sl] = [(cV, t6)] if not tbl_evac else (tbl_free[sl] + [(cV, t6)])


        for t in range(TPS):
            sl = t % 2
            w = pr.op("sync", "dma_start", XI[:, sl * 1024:(sl + 1) * 1024],
                      xin[s, t * P:(t + 1) * P, :],
                      waits=list(xi_free[sl]), inc=cXs[sl], amt=16)
            xi_free[sl] = [(cXs[sl], w)]
            ap1 = pr.op("vector", "tensor_scalar",
                        LNT[:, sl * 1024:(sl + 1) * 1024],
                        XI[:, sl * 1024:(sl + 1) * 1024],
                        SVY[:, t:t + 1], SVY[:, TPS + t:TPS + t + 1],
                        ALU.subtract, ALU.mult,
                        waits=[(cXs[sl], w), svy_r[t]] + list(lnt_free[sl]),
                        inc=cV)
            xi_free[sl] = [(cV, ap1)]
            psl1 = PPt if sl == 0 else PPt2
            pi1 = 0 if sl == 0 else 1
            tp1 = None
            for dc_ in range(DC):
                tp1 = pr.op("tensor", "transpose",
                            psl1[:, dc_ * P:(dc_ + 1) * P],
                            LNT[:, sl * 1024 + dc_ * P: sl * 1024 + (dc_ + 1) * P],
                            IDT[:],
                            waits=([(cV, ap1), (cC, CONSTW)] + list(ppt_free[pi1])) if dc_ == 0 else [],
                            inc=cT if dc_ == DC - 1 else None)
            lnt_free[sl] = [(cT, tp1)]
            info = (psl1, pi1, tp1, lv[:, :, t * P:(t + 1) * P],
                    list(lnfm_free) if t == 0 else [])
            if t == 0:
                lnfm_free = []
            if pend is not None:
                ln1_ev[t - 1] = ln_evac(pend)
                if t - 1 >= 8 and (t - 1) % 2 == 0:
                    emit_pair(KFM, 8192, [wk_rdy], (t - 1 - 8) // 2, 0, tbl_evac=True)
                emit_v(t - 1, ln1_ev[t - 1])
            pend = info
        ln1_ev[TPS - 1] = ln_evac(pend)
        emit_v(TPS - 1, ln1_ev[TPS - 1])

        if dbg and s == 0:
            pr.op("gpsimd", "dma_start", dbg["dbg_lnfm"][:], LNFM[:, :],
                  waits=[(cS, ln1_ev[-1])], inc=cDB, amt=16)
            lnfm_free = lnfm_free + [(cDB, cDB.n)]
            pr.op("gpsimd", "dma_start", dbg["dbg_v"][:], VTM[:, :],
                  waits=[(cS, v_done)], inc=cDB, amt=16)
            vtm_free = vtm_free + [(cDB, cDB.n)]

        wv_done_ct = cT.n
        for c in range(4):
            emit_pair(KFM, 8192, [wk_rdy], c, 1)
        wq_src = dr["wqp"].rearrange("(c p) (g q) -> p c g q", p=P, g=8)
        wq_dst = WBUF[:, 0:8192].rearrange("p (c g q) -> p c g q", c=DC, g=8)
        wq_rdy_g = []
        for g in (0, 4, 1, 5, 2, 6, 3, 7):
            sem = cWDs[wd_i[0] % 2]
            wd_i[0] += 1
            w = pr.op("gpsimd", "dma_start",
                      wq_dst[:, :, g, :], wq_src[:, :, g, :],
                      waits=[(sem, sem.n)] + ([(cT, wv_done_ct)] if g == 0 else []),
                      inc=sem, amt=16)
            wq_rdy_g.append((g, (sem, w)))
        wq_rdy_g.sort()
        wq_rdy_c = [[wq_rdy_g[c][1], wq_rdy_g[c + 4][1]] for c in range(4)]
        wo_rdy = load_weight_bf16(
            dr["wot"].rearrange("(c p) n -> p c n", p=P),
            WBUF, 8192, pre=[(cT, cT.n)])
        for h in range(2):
            for c in range(4):
                emit_pair(QFM, 0, wq_rdy_c[c], c, h)
        q_done = cV.n
        ab_free = [(cV, q_done)]

        if dbg and s == 0:
            pr.op("gpsimd", "dma_start", dbg["dbg_q"][:], QFM[:, :],
                  waits=[(cV, q_done)], inc=cDB, amt=16)
            qfm_free = qfm_free + [(cDB, cDB.n)]
            pr.op("gpsimd", "dma_start", dbg["dbg_k"][:], KFM[:, :],
                  waits=[(cV, q_done)], inc=cDB, amt=16)
            kfm_free = kfm_free + [(cDB, cDB.n)]

        qk_mm_done = cT.n
        w2_rdy = [(cS, 0)] * E
        if s == 0:
            pc["w1"] = []
            for e in range(E):
                src = dr["w1l"][e].rearrange("(c p) n -> p c n", p=P)
                dst = dr["w1b"][e].rearrange("p (c n) -> p c n", n=H)
                pc["w1"].append(precast(dst, src))
            for e in range(6):
                dst = LNFM if e < 4 else WBUF
                col = (e % 4) * 4096
                src = dr["w2l"][e].rearrange("(c p) n -> p c n", p=P)
                w2_rdy[e] = load_weight_bf16(
                    src, dst, col,
                    pre=([(cT, qk_mm_done)] + (list(lnfm_free) if e == 0 else [])))
        lnfm_free = []
        wslot_free = [[], []]

        def w1slot(e):
            wsl = e % 3
            if wsl == 0:
                return TBL, 0
            return VTM, 8192 + (wsl - 1) * 4096

        def load_w1(e, pre):
            dst, col = w1slot(e)
            return load_plain(dst, col, 4096, dr["w1b"][e],
                              pre=list(pre) + [pc["w1"][e]])

        w1pre = {}
        tblw_saved = tbl_free[0] + tbl_free[1]
        tbl_free = [[], []]

        def emit_wo_transp(t):
            sl = t % 2
            w = pr.op("sync", "dma_start", XI[:, sl * 1024:(sl + 1) * 1024],
                      xin[s, t * P:(t + 1) * P, :],
                      waits=list(xi_free[sl]), inc=cXs[sl], amt=16)
            tp = None
            for dc_ in range(DC):
                tp = pr.op("tensor", "transpose", PPt[:, dc_ * P:(dc_ + 1) * P],
                           LNT[:, dc_ * P:(dc_ + 1) * P], IDT[:],
                           waits=(list(ppt_free[0]) + list(avev[t])) if dc_ == 0 else [],
                           inc=cT if dc_ == DC - 1 else None)
            lnt_free[0] = [(cT, tp)]
            ev = pr.op("vector", "tensor_copy", LNT[:, 1024:2048], PPt[:, :],
                       waits=[(cT, tp)] + list(lnt_free[1]), inc=cV)
            ppt_free[0] = [(cV, ev)]
            return (sl, w, ev)

        def emit_ln2stats(t):
            sl = t % 2
            ad = adres[t]
            par = t % 2
            cb = 32 + 6 * par
            c2_ss, c2_s, c2_b, c2_t = cb, cb + 1, cb + 2, cb + 3
            sq2 = pr.op("scalar", "activation", XI[:, sl * 1024:(sl + 1) * 1024],
                        XO[:, sl * 1024:(sl + 1) * 1024], AF.Square,
                        waits=[(cV, ad)] + list(svx_free[par]), inc=cS,
                        accum_out=SVX[:, c2_ss:c2_ss + 1])
            xi_free[sl] = [(cV, ad), (cS, sq2)]
            sm2 = pr.op("vector", "reduce_sum", SVX[:, c2_s:c2_s + 1],
                        XO[:, sl * 1024:(sl + 1) * 1024], AX,
                        waits=[(cV, ad)] + list(svx_free[par]), inc=cV)
            b2_ = pr.op("vector", "tensor_scalar", SVX[:, c2_b:c2_b + 1],
                        SVX[:, c2_ss:c2_ss + 1], 1.0 / D, 1e-5, ALU.mult,
                        ALU.add, waits=[(cS, sq2), (cV, sm2)], inc=cV)
            m2_ = pr.op("scalar", "activation", SVX[:, t:t + 1],
                        SVX[:, c2_s:c2_s + 1], AF.Copy, scale=1.0 / D,
                        waits=[(cV, sm2)] + list(svx_free[par]), inc=cS)
            mm2 = pr.op("scalar", "activation", SVX[:, c2_t:c2_t + 1],
                        SVX[:, c2_s:c2_s + 1], AF.Square, scale=1.0 / D,
                        inc=cS)
            ln2_ = pr.op("scalar", "activation", SVX[:, c2_t:c2_t + 1],
                         SVX[:, c2_t:c2_t + 1], AF.Ln, scale=-1.0,
                         bias=SVX[:, c2_b:c2_b + 1],
                         waits=[(cS, mm2), (cV, b2_)], inc=cS)
            r2_ = pr.op("scalar", "activation", SVX[:, TPS + t:TPS + t + 1],
                        SVX[:, c2_t:c2_t + 1], AF.Exp, scale=-0.5,
                        waits=[(cS, ln2_)], inc=cS)
            svx_free[par] = [(cS, r2_)]
            svx_r[t] = (cS, r2_)

        def emit_wo_mm(t, st1):
            sl, w, ev = st1
            mm = None
            for dc_ in range(DC):
                for n2 in range(2):
                    first = dc_ == 0 and n2 == 0
                    mm = pr.op(
                        "tensor", "matmul", PPo[:, n2 * 512:(n2 + 1) * 512],
                        LNT[:, 1024 + dc_ * P: 1024 + (dc_ + 1) * P],
                        WBUF[:, 8192 + dc_ * 1024 + n2 * 512: 8192 + dc_ * 1024 + (n2 + 1) * 512],
                        start=(dc_ == 0), stop=(dc_ == DC - 1),
                        waits=([(cV, ev), wo_rdy] + list(ppo_free)) if first else [],
                        inc=cT if (dc_ == DC - 1 and n2 == 1) else None)
            lnt_free[1] = [(cT, mm)]
            return (sl, w, mm)

        def emit_wo_fin(t, st2):
            sl, w, mm = st2
            ad = pr.op("vector", "tensor_tensor", XO[:, sl * 1024:(sl + 1) * 1024],
                       XI[:, sl * 1024:(sl + 1) * 1024], PPo[:, :], ALU.add,
                       waits=[(cT, mm), (cXs[sl], w)] + list(xo_free[sl]), inc=cV)
            st = pr.op("scalar", "dma_start", xres[s, t * P:(t + 1) * P, :],
                       XO[:, sl * 1024:(sl + 1) * 1024], waits=[(cV, ad)],
                       inc=cSTs[sl], amt=16)
            xo_free[sl] = [(cSTs[sl], st)]
            xres_st[s * TPS + t] = (cSTs[sl], st)
            return ad

        avev = [None] * TPS
        adres = [None] * TPS
        for qt in range(TPS):
            if qt == 6:
                w1pre[0] = load_w1(0, pre=tblw_saved)
            if s == 1 and qt in (1, 3, 5, 7, 9, 11):
                e_ = qt // 2
                dst = LNFM if e_ < 4 else WBUF
                w2_rdy[e_] = load_plain(
                    dst, (e_ % 4) * 4096, 4096, dr["w2b"][e_],
                    pre=[(cT, qk_mm_done), pc["w2"][e_]] + (list(lnfm_free) if e_ == 0 else []))
            par = qt % 2
            b0 = 40 + 8 * par
            c_p0, c_p1, c_p2, c_p3, c_nm, c_l, c_rl, c_l1 = range(b0, b0 + 8)
            cps = [c_p0, c_p1, c_p2, c_p3]

            def score_kb(kb):
                mm = None
                for dc_ in range(DC):
                    mm = pr.op(
                        "tensor", "matmul", PQBK[kb][:, 0:512],
                        QFM[:, dc_ * 2048 + qt * P: dc_ * 2048 + (qt + 1) * P],
                        KFM[:, dc_ * 2048 + kb * 512: dc_ * 2048 + (kb + 1) * 512],
                        start=(dc_ == 0), stop=(dc_ == DC - 1),
                        waits=(([(cV, q_done)] if kb == 0 else []) + pqf[kb]) if dc_ == 0 else [],
                        inc=cT if dc_ == DC - 1 else None)
                return mm

            kbdone = [None] * 4
            kbdone[0] = score_kb(0)
            kbdone[1] = score_kb(1)
            st1 = emit_wo_transp(qt - 1) if qt > 0 else None
            kbdone[2] = score_kb(2)
            kbdone[3] = score_kb(3)
            nm = None
            for kb in range(4):
                nmk = pr.op("vector", "reduce_max", SV[:, cps[kb]:cps[kb] + 1],
                            PQBK[kb][:, 0:512], AX, negate=True,
                            waits=[(cT, kbdone[kb])] + (list(svq_free[par]) if kb == 0 else []),
                            inc=cV)
                if kb > 0:
                    acc = cps[0] if kb == 1 else c_nm
                    nm = pr.op("vector", "tensor_tensor", SV[:, c_nm:c_nm + 1],
                               SV[:, acc:acc + 1], SV[:, cps[kb]:cps[kb] + 1],
                               ALU.min, waits=[(cV, nmk)], inc=cV)
            exs = []
            ex_ = None
            for kb in range(4):
                ex_ = pr.op("scalar", "activation", AB[:, kb * 512:(kb + 1) * 512],
                            PQBK[kb][:, 0:512],
                            AF.Exp, bias=SV[:, c_nm:c_nm + 1], scale=1.0,
                            waits=([(cV, nm)] + list(ab_free)) if kb == 0 else [],
                            inc=cS)
                exs.append(ex_)
            rs0 = pr.op("vector", "reduce_sum", SV[:, c_l:c_l + 1],
                        AB[:, 0:1024], AX, waits=[(cS, exs[1])], inc=cV)
            rs1 = pr.op("vector", "reduce_sum", SV[:, c_l1:c_l1 + 1],
                        AB[:, 1024:2048], AX, waits=[(cS, exs[3])], inc=cV)
            lsm = pr.op("vector", "tensor_tensor", SV[:, c_l:c_l + 1],
                        SV[:, c_l:c_l + 1], SV[:, c_l1:c_l1 + 1], ALU.add,
                        waits=[(cV, rs1)], inc=cV)
            rc = pr.op("vector", "reciprocal", SV[:, c_rl:c_rl + 1],
                       SV[:, c_l:c_l + 1], waits=[(cV, lsm)], inc=cV)
            svq_free[par] = [(cV, rc)]
            st2 = emit_wo_mm(qt - 1, st1) if qt > 0 else None
            if qt > 0:
                sl_, w_, mm_ = st2
                ad_prev = pr.op("vector", "tensor_tensor",
                                XO[:, sl_ * 1024:(sl_ + 1) * 1024],
                                XI[:, sl_ * 1024:(sl_ + 1) * 1024], PPo[:, :],
                                ALU.add,
                                waits=[(cT, mm_), (cXs[sl_], w_)] + list(xo_free[sl_]),
                                inc=cV)
                adres[qt - 1] = ad_prev
                ppo_free = [(cV, ad_prev)]
            evls = []
            for g in range(2):
                pt = PPt if g == 0 else PPt2
                tp = None
                for j in range(8):
                    kt = g * 8 + j
                    tp = pr.op("tensor", "transpose", pt[:, j * P:(j + 1) * P],
                               AB[:, kt * P:(kt + 1) * P], IDT[:],
                               waits=([(cS, exs[2 * g + 1])] + list(ppt_free[g])) if j == 0 else [],
                               inc=cT if j == 7 else None)
                evl = pr.op("scalar", "copy", AB[:, g * 1024:(g + 1) * 1024],
                            pt[:, :],
                            waits=[(cT, tp), (cV, rs0 if g == 0 else rs1)],
                            inc=cS)
                evls.append(evl)
                ppt_free[g] = [(cS, evl)]
            if qt > 0:
                sl_, w_, mm_ = st2
                stx = pr.op("scalar", "dma_start",
                            xres[s, (qt - 1) * P:qt * P, :],
                            XO[:, sl_ * 1024:(sl_ + 1) * 1024],
                            waits=[(cV, adres[qt - 1])], inc=cSTs[sl_], amt=16)
                xo_free[sl_] = [(cSTs[sl_], stx)]
                xres_st[s * TPS + qt - 1] = (cSTs[sl_], stx)
            if qt > 1:
                emit_ln2stats(qt - 2)
            pqf[0] = [(cS, exs[0])]
            pqf[1] = [(cS, exs[1])]
            pqf[2] = [(cS, exs[2])]
            pqf[3] = [(cS, exs[3])]
            mm = None
            for n2 in range(2):
                for kt in range(TPS):
                    first = n2 == 0 and kt == 0
                    mm = pr.op(
                        "tensor", "matmul", PPo[:, n2 * 512:(n2 + 1) * 512],
                        AB[:, kt * P:(kt + 1) * P],
                        VTM[:, kt * 1024 + n2 * 512: kt * 1024 + (n2 + 1) * 512],
                        start=(kt == 0), stop=(kt == TPS - 1),
                        waits=([(cS, v_done), (cS, evls[0])] + list(ppo_free)) if first else ([(cS, evls[1])] if (n2 == 0 and kt == 8) else []),
                        inc=cT if (n2 == 1 and kt == TPS - 1) else None)
            ab_free = [(cT, mm)]
            av = pr.op("vector", "tensor_scalar_mul", LNT[:, 0:1024], PPo[:, :],
                       SV[:, c_rl:c_rl + 1],
                       waits=[(cT, mm), (cV, rc)] + list(lnt_free[0]), inc=cV)
            ppo_free = [(cV, av)]
            avev[qt] = [(cV, av)]
        st1 = emit_wo_transp(TPS - 1)
        ad_last = emit_wo_fin(TPS - 1, emit_wo_mm(TPS - 1, st1))
        adres[TPS - 1] = ad_last
        ppo_free = [(cV, ad_last)]
        emit_ln2stats(TPS - 2)
        emit_ln2stats(TPS - 1)
        vtmhi_free = [(cT, cT.n), (cDB, cDB.n)]
        for e_ in (1, 2):
            dst_, col_ = w1slot(e_)
            w1pre[e_] = load_plain(dst_, col_, 4096, dr["w1b"][e_],
                                   pre=list(vtmhi_free) + [pc["w1"][e_]],
                                   eng="scalar")
        for e in (6, 7):
            if s == 0:
                src = dr["w2l"][e].rearrange("(c p) n -> p c n", p=P)
                w2_rdy[e] = load_weight_bf16(src, WBUF, (e - 4) * 4096,
                                             pre=[(cT, cT.n)])
            else:
                w2_rdy[e] = load_plain(WBUF, (e - 4) * 4096, 4096,
                                       dr["w2b"][e],
                                       pre=[(cT, cT.n), pc["w2"][e]])
        if s == 0:
            tb_last = None
            for dstnm, srcnm in (("cbx", "cosb"), ("sbx", "sinb"),
                                 ("cqx", "cosq"), ("sqx", "sinq")):
                tb_last = precast(dr[dstnm][:], dr[srcnm][:], chain=0)
            pc["tb"] = tb_last
            pc["w2"] = []
            for e in range(E):
                src = dr["w2l"][e].rearrange("(c p) n -> p c n", p=P)
                dst = dr["w2b"][e].rearrange("p (c n) -> p c n", n=1024)
                pc["w2"].append(precast(dst, src, chain=0))

        lnb = VTM[:, 0:8192].rearrange("p (c s2) -> p c s2", c=DC)
        vtm_free = [(cT, cT.n), (cDB, cDB.n)]
        w1slot_free = [[], list(vtmhi_free), list(vtmhi_free)]
        for b in range(NBLK):
            ev_last = None

            def emit_gating2(pairs):
                ctx = []
                for bt, ev in pairs:
                    sl = bt
                    gb = 46 * bt
                    ctx.append(dict(
                        bt=bt, ev=ev, sl=sl,
                        LG=SVG[:, gb:gb + 8], L2=SVG[:, gb + 8:gb + 16],
                        M1=SVG[:, gb + 16:gb + 24], M2=SVG[:, gb + 24:gb + 32],
                        GT=SVG[:, gb + 32:gb + 40],
                        sc=list(range(gb + 40, gb + 46))))
                S_ = lambda c: SVG[:, c:c + 1]
                for x in ctx:
                    gm = None
                    for dc_ in range(DC):
                        gm = pr.op("tensor", "matmul", PQ0a[:, 0:8],
                                   lnb[:, dc_, x["bt"] * P:(x["bt"] + 1) * P],
                                   GWSB[:, dc_ * 8:(dc_ + 1) * 8],
                                   start=(dc_ == 0), stop=(dc_ == DC - 1),
                                   waits=([(cS, x["ev"]), (cC, CONSTW), (cWD, gww)] + pqf[0]) if dc_ == 0 else [],
                                   inc=cT if dc_ == DC - 1 else None)
                    lg = pr.op("scalar", "copy", x["LG"], PQ0a[:, 0:8],
                               waits=[(cT, gm)] + list(svg_free[x["sl"]]), inc=cS)
                    pqf[0] = [(cS, lg)]
                    x["lg"] = lg
                for x in ctx:
                    x["g1"] = pr.op("vector", "tensor_tensor", x["LG"], x["LG"],
                                    GBB[:], ALU.add, waits=[(cS, x["lg"])], inc=cV)
                for x in ctx:
                    x["g2"] = pr.op("vector", "reduce_max", S_(x["sc"][0]), x["LG"],
                                    AX, waits=[(cV, x["g1"])], inc=cV)
                for x in ctx:
                    x["g3"] = pr.op("vector", "tensor_single_scalar", x["M1"],
                                    x["LG"], S_(x["sc"][0]), ALU.is_equal,
                                    waits=[(cV, x["g2"])], inc=cV)
                for x in ctx:
                    x["g4"] = pr.op("vector", "scalar_tensor_tensor", x["L2"],
                                    x["M1"], -1e30, x["LG"], ALU.mult, ALU.add,
                                    waits=[(cV, x["g3"])], inc=cV)
                for x in ctx:
                    x["g5"] = pr.op("vector", "reduce_max", S_(x["sc"][1]),
                                    x["L2"], AX, waits=[(cV, x["g4"])], inc=cV)
                for x in ctx:
                    x["g6"] = pr.op("vector", "tensor_single_scalar", x["M2"],
                                    x["L2"], S_(x["sc"][1]), ALU.is_equal,
                                    waits=[(cV, x["g5"])], inc=cV)
                for x in ctx:
                    x["g7"] = pr.op("vector", "tensor_tensor", S_(x["sc"][2]),
                                    S_(x["sc"][1]), S_(x["sc"][0]), ALU.subtract,
                                    waits=[(cV, x["g5"])], inc=cV)
                for x in ctx:
                    x["g8"] = pr.op("scalar", "activation", S_(x["sc"][3]),
                                    S_(x["sc"][2]), AF.Exp,
                                    waits=[(cV, x["g7"])], inc=cS)
                for x in ctx:
                    x["g9"] = pr.op("vector", "tensor_scalar_add", S_(x["sc"][4]),
                                    S_(x["sc"][3]), 1.0, waits=[(cS, x["g8"])],
                                    inc=cV)
                for x in ctx:
                    x["g10"] = pr.op("vector", "reciprocal", S_(x["sc"][5]),
                                     S_(x["sc"][4]), waits=[(cV, x["g9"])], inc=cV)
                for x in ctx:
                    x["g11"] = pr.op("vector", "scalar_tensor_tensor", x["GT"],
                                     x["M2"], S_(x["sc"][3]), x["M1"], ALU.mult,
                                     ALU.add, waits=[(cV, x["g6"]), (cS, x["g8"])],
                                     inc=cV)
                for x in ctx:
                    x["g12"] = pr.op("vector", "tensor_scalar_mul", x["GT"],
                                     x["GT"], S_(x["sc"][5]),
                                     waits=[(cV, x["g10"]), (cV, x["g11"])], inc=cV)
                for x in ctx:
                    g13 = pr.op("vector", "tensor_copy",
                                GTB[:, x["bt"] * 8:(x["bt"] + 1) * 8], x["GT"],
                                waits=[(cV, x["g12"])] + (gtb_free if x["bt"] == 0 else []),
                                inc=cV)
                    svg_free[x["sl"]] = [(cV, g13)]

            pend2 = None
            evs2 = []
            for bt in range(BT):
                t = b * BT + bt
                sl = bt % 2
                w = pr.op("sync", "dma_start", XI[:, sl * 1024:(sl + 1) * 1024],
                          xres[s, t * P:(t + 1) * P, :],
                          waits=[xres_st[s * TPS + t]] + list(xi_free[sl]),
                          inc=cXs[sl], amt=16)
                xi_free[sl] = [(cXs[sl], w)]
                ap2 = pr.op("vector", "tensor_scalar",
                            LNT[:, sl * 1024:(sl + 1) * 1024],
                            XI[:, sl * 1024:(sl + 1) * 1024],
                            SVX[:, t:t + 1], SVX[:, TPS + t:TPS + t + 1],
                            ALU.subtract, ALU.mult,
                            waits=[(cXs[sl], w), svx_r[t]] + list(lnt_free[sl]),
                            inc=cV)
                xi_free[sl] = [(cV, ap2)]
                psl2 = PPt if sl == 0 else PPt2
                pi2 = 0 if sl == 0 else 1
                tp2 = None
                for dc_ in range(DC):
                    tp2 = pr.op("tensor", "transpose",
                                psl2[:, dc_ * P:(dc_ + 1) * P],
                                LNT[:, sl * 1024 + dc_ * P: sl * 1024 + (dc_ + 1) * P],
                                IDT[:],
                                waits=([(cV, ap2), (cC, CONSTW)] + list(ppt_free[pi2])) if dc_ == 0 else [],
                                inc=cT if dc_ == DC - 1 else None)
                lnt_free[sl] = [(cT, tp2)]
                info = (psl2, pi2, tp2, lnb[:, :, bt * P:(bt + 1) * P],
                        list(vtm_free) if bt == 0 else [])
                if bt == 0:
                    vtm_free = []
                if pend2 is not None:
                    evs2.append((bt - 1, ln_evac(pend2)))
                pend2 = info
            ev_last = ln_evac(pend2)
            evs2.append((BT - 1, ev_last))
            emit_gating2(evs2)
            evs2 = []
            g13_last = cV.n
            ab_free_in = list(ab_free)

            def emit_gates_tail():
                nonlocal gtb_free
                gtp = None
                for bt in range(BT):
                    gtp = pr.op("tensor", "transpose",
                                PPt2[0:8, bt * P:(bt + 1) * P],
                                GTB[:, bt * 8:(bt + 1) * 8], IDT[:],
                                waits=([(cV, g13_last)] + list(ppt_free[1])) if bt == 0 else [],
                                inc=cT if bt == BT - 1 else None)
                gtb_free = [(cT, gtp)]
                gfm_ = pr.op("scalar", "copy", AB[0:E, b * 1024:(b + 1) * 1024],
                             PPt2[0:E, :], waits=[(cT, gtp)] + ab_free_in,
                             inc=cS)
                ppt_free[1] = [(cS, gfm_)]
                return gfm_
            ln2_done = cS.n

            if dbg and s == 0 and b == 0:
                pr.op("gpsimd", "dma_start", dbg["dbg_lnb"][:], VTM[:, 0:8192],
                      waits=[(cS, ln2_done)], inc=cDB, amt=16)
                vtm_free = vtm_free + [(cDB, cDB.n)]

            ghA_done = 0
            ph_free = {0: pqf[0] + pqf[1], 1: pqf[2] + pqf[3]}
            ABH = AB[:, (1 - b) * 1024:(2 - b) * 1024]

            def w1mms(e, hc, first_w, e_rdy, w1t, w1base):
                mm = None
                for dc_ in range(DC):
                    for n2 in range(2):
                        first = dc_ == 0 and n2 == 0
                        mm = pr.op(
                            "tensor", "matmul", PQBK[(hc % 2) * 2 + n2][:, 0:512],
                            w1t[:, w1base + dc_ * 512 + hc * P: w1base + dc_ * 512 + (hc + 1) * P],
                            VTM[:, dc_ * 1024 + n2 * 512: dc_ * 1024 + (n2 + 1) * 512],
                            start=(dc_ == 0), stop=(dc_ == DC - 1),
                            waits=([e_rdy] + first_w) if first else [],
                            inc=cT if (dc_ == DC - 1 and n2 == 1) else None)
                return mm

            def e0_gelu(hc, mm, buf, extra):
                ge1_ = pr.op("scalar", "activation", buf[:, 0:512],
                             PQBK[(hc % 2) * 2][:, 0:512], AF.Gelu,
                             bias=B1SB[:, hc:hc + 1],
                             waits=[(cT, mm), (cC, CONSTW)] + extra, inc=cS)
                ge2_ = pr.op("scalar", "activation", buf[:, 512:1024],
                             PQBK[(hc % 2) * 2 + 1][:, 0:512], AF.Gelu,
                             bias=B1SB[:, hc:hc + 1], inc=cS)
                return ge2_

            e_rdy = w1pre.pop(0)
            w1t, w1base = w1slot(0)
            mm0 = w1mms(0, 0, [(cS, ev_last)] + ph_free[0], e_rdy, w1t, w1base)
            ge0 = e0_gelu(0, mm0, LNT[:, 0:1024], list(lnt_free[0]))
            mm1 = w1mms(0, 1, ph_free[1], e_rdy, w1t, w1base)
            ge1 = e0_gelu(1, mm1, ABH, list(ab_free))
            mm2 = w1mms(0, 2, [(cS, ge0)], e_rdy, w1t, w1base)
            mm3 = w1mms(0, 3, [(cS, ge1)], e_rdy, w1t, w1base)
            gfm = emit_gates_tail()
            bc = None
            for n2 in range(2):
                bc = pr.op("tensor", "matmul",
                           PPo[:, n2 * 512:(n2 + 1) * 512],
                           SEL8[:, 0:P],
                           AB[0:E, b * 1024 + n2 * 512: b * 1024 + (n2 + 1) * 512],
                           start=True, stop=True,
                           waits=([(cS, gfm), (cC, CONSTW)] + list(ppo_free)) if n2 == 0 else [],
                           inc=cT if n2 == 1 else None)
            bce = pr.op("scalar", "copy", LNT[:, 1024:2048], PPo[:, :],
                        waits=[(cT, bc)] + list(lnt_free[1]), inc=cS)
            ppo_free = [(cS, bce)]
            gh0 = pr.op("vector", "tensor_tensor", QFM[:, 0:1024],
                        LNT[:, 0:1024], LNT[:, 1024:2048], ALU.mult,
                        waits=[(cS, bce)] + list(qfm_free), inc=cV)
            qfm_free = []
            ge2 = e0_gelu(2, mm2, LNT[:, 0:1024], [(cV, gh0)])
            gh1 = pr.op("vector", "tensor_tensor", QFM[:, 1024:2048],
                        ABH, LNT[:, 1024:2048], ALU.mult, inc=cV)
            ge3 = e0_gelu(3, mm3, ABH, [(cV, gh1)])
            gh2 = pr.op("vector", "tensor_tensor", QFM[:, 2048:3072],
                        LNT[:, 0:1024], LNT[:, 1024:2048], ALU.mult,
                        waits=[(cS, ge2)], inc=cV)
            gh3 = pr.op("vector", "tensor_tensor", QFM[:, 3072:4096],
                        ABH, LNT[:, 1024:2048], ALU.mult,
                        waits=[(cS, ge3)], inc=cV)
            ph_free[0] = [(cS, ge2)]
            ph_free[1] = [(cS, ge3)]
            lnt_free[0] = [(cV, gh2)]
            lnt_free[1] = [(cV, gh3)]
            abh_free = [(cV, gh3)]
            w1slot_free[0] = [(cT, cT.n)]
            ghA_done = gh3

            for e in range(1, E):
                wsl = e % 3
                if e in w1pre:
                    e_rdy = w1pre.pop(e)
                else:
                    e_rdy = load_w1(e, pre=list(w1slot_free[wsl]))
                w1t, w1base = w1slot(e)
                bc = None
                for n2 in range(2):
                    bc = pr.op("tensor", "matmul",
                               PPo[:, n2 * 512:(n2 + 1) * 512],
                               SEL8[:, e * P:(e + 1) * P],
                               AB[0:E, b * 1024 + n2 * 512: b * 1024 + (n2 + 1) * 512],
                               start=True, stop=True,
                               waits=([(cS, gfm), (cC, CONSTW)] + list(ppo_free)) if n2 == 0 else [],
                               inc=cT if n2 == 1 else None)
                bce = pr.op("scalar", "copy", LNT[:, 1024:2048], PPo[:, :],
                            waits=[(cT, bc)] + list(lnt_free[1]), inc=cS)
                ppo_free = [(cS, bce)]
                gh_done = None
                for hc in range(HCN):
                    par = hc % 2
                    mm = None
                    for dc_ in range(DC):
                        for n2 in range(2):
                            first = dc_ == 0 and n2 == 0
                            mm = pr.op(
                                "tensor", "matmul", PQBK[par * 2 + n2][:, 0:512],
                                w1t[:, w1base + dc_ * 512 + hc * P: w1base + dc_ * 512 + (hc + 1) * P],
                                VTM[:, dc_ * 1024 + n2 * 512: dc_ * 1024 + (n2 + 1) * 512],
                                start=(dc_ == 0), stop=(dc_ == DC - 1),
                                waits=([e_rdy, (cS, ev_last)] + list(ph_free[par])) if first else [],
                                inc=cT if (dc_ == DC - 1 and n2 == 1) else None)
                    ge1 = pr.op("scalar", "activation", LNT[:, 0:512],
                                PQBK[par * 2][:, 0:512], AF.Gelu,
                                bias=B1SB[:, e * HCN + hc: e * HCN + hc + 1],
                                waits=[(cT, mm), (cC, CONSTW)] + list(lnt_free[0]),
                                inc=cS)
                    ge = pr.op("scalar", "activation", LNT[:, 512:1024],
                               PQBK[par * 2 + 1][:, 0:512], AF.Gelu,
                               bias=B1SB[:, e * HCN + hc: e * HCN + hc + 1],
                               inc=cS)
                    ph_free[par] = [(cS, ge)]
                    GHdst = QFM if e < 4 else KFM
                    gcol = ((e % 4) * 4 + hc) * 1024
                    pre = []
                    if e == 0 and hc == 0:
                        pre = list(qfm_free)
                        qfm_free = []
                    elif e == 4 and hc == 0:
                        pre = list(kfm_free)
                        kfm_free = []
                    gh_done = pr.op("vector", "tensor_tensor",
                                    GHdst[:, gcol:gcol + 1024], LNT[:, 0:1024],
                                    LNT[:, 1024:2048], ALU.mult,
                                    waits=[(cS, ge)] + pre, inc=cV)
                    lnt_free[0] = [(cV, gh_done)]
                w1slot_free[wsl] = [(cT, cT.n)]
                lnt_free[1] = [(cV, gh_done)]
                ghA_done = gh_done
            pqf[0] = list(ph_free[0])
            pqf[1] = list(ph_free[0])
            pqf[2] = list(ph_free[1])
            pqf[3] = list(ph_free[1])
            if b + 1 < NBLK:
                for e_ in range(3):
                    dst_, col_ = w1slot(e_)
                    w1pre[e_] = load_plain(
                        dst_, col_, 4096, dr["w1b"][e_],
                        pre=list(w1slot_free[e_ % 3]) + [pc["w1"][e_]],
                        eng="scalar")

            if dbg and s == 0 and b == 0:
                pr.op("gpsimd", "dma_start", dbg["dbg_gfm"][:, 0:1024],
                      AB[0:E, 0:1024], waits=[(cS, gfm), (cDB, cDB.n)],
                      inc=cDB, amt=16)
                pr.op("gpsimd", "dma_start", dbg["dbg_gh"][:], QFM[:, :],
                      waits=[(cV, ghA_done), (cDB, cDB.n)], inc=cDB, amt=16)
                ab_free = ab_free + [(cDB, cDB.n)]
                qfm_free = qfm_free + [(cDB, cDB.n)]

            for bt in range(BT):
                t = b * BT + bt
                sl = bt % 2
                w = pr.op("sync", "dma_start", XI[:, sl * 1024:(sl + 1) * 1024],
                          xres[s, t * P:(t + 1) * P, :],
                          waits=[xres_st[s * TPS + t]] + list(xi_free[sl]),
                          inc=cXs[sl], amt=16)
                if b == 0 and s + 1 < NSEQ:
                    prefetch_ln1_stats(s + 1, tiles=(2 * bt, 2 * bt + 1))
                use_ppo = bt % 2 == 0
                fr = ppo_free if use_ppo else (pqf[0] + pqf[1])
                mm = None
                for n2 in range(2):
                    po = PPo[:, n2 * 512:(n2 + 1) * 512] if use_ppo else PQBK[n2][:, 0:512]
                    for e in range(E):
                        GHsrc = QFM if e < 4 else KFM
                        w2src = LNFM if e < 4 else WBUF
                        for hc in range(HCN):
                            gcol = ((e % 4) * 4 + hc) * 1024 + bt * P
                            wcol = (e % 4) * 4096 + hc * 1024 + n2 * 512
                            first = n2 == 0 and e == 0 and hc == 0
                            ew = []
                            if first:
                                ew = [(cV, ghA_done)] + list(fr)
                            if bt == 0 and b == 0 and n2 == 0 and hc == 0:
                                ew = ew + [w2_rdy[e]]
                            mm = pr.op(
                                "tensor", "matmul", po,
                                GHsrc[:, gcol:gcol + P],
                                w2src[:, wcol:wcol + 512],
                                start=(e == 0 and hc == 0), stop=False,
                                waits=ew)
                    mm = pr.op("tensor", "matmul", po,
                               AB[0:E, b * 1024 + bt * P: b * 1024 + (bt + 1) * P],
                               B2SB[:, n2 * 512:(n2 + 1) * 512],
                               start=False, stop=True,
                               waits=[b2c] if n2 == 0 else [],
                               inc=cT if n2 == 1 else None)
                if use_ppo:
                    ad = pr.op("vector", "tensor_tensor",
                               XO[:, sl * 1024:(sl + 1) * 1024],
                               XI[:, sl * 1024:(sl + 1) * 1024], PPo[:, :],
                               ALU.add,
                               waits=[(cT, mm), (cXs[sl], w)] + list(xo_free[sl]),
                               inc=cV)
                    ppo_free = [(cV, ad)]
                else:
                    pr.op("vector", "tensor_tensor",
                          XO[:, sl * 1024:sl * 1024 + 512],
                          XI[:, sl * 1024:sl * 1024 + 512], PQ0a[:, 0:512],
                          ALU.add,
                          waits=[(cT, mm), (cXs[sl], w)] + list(xo_free[sl]),
                          inc=cV)
                    ad = pr.op("vector", "tensor_tensor",
                               XO[:, sl * 1024 + 512:(sl + 1) * 1024],
                               XI[:, sl * 1024 + 512:(sl + 1) * 1024],
                               PQ0b[:, 0:512], ALU.add, inc=cV)
                    pqf[0] = [(cV, ad)]
                    pqf[1] = [(cV, ad)]
                xi_free[sl] = [(cV, ad)]
                st = pr.op("scalar", "dma_start", yout[s, t * P:(t + 1) * P, :],
                           XO[:, sl * 1024:(sl + 1) * 1024], waits=[(cV, ad)],
                           inc=cSTs[sl], amt=16)
                xo_free[sl] = [(cSTs[sl], st)]
            ab_free = [(cT, cT.n)]
        lnfm_free = [(cT, cT.n)]
        vtm_free = [(cT, cT.n)]
        wslot_free = [[(cT, cT.n)], [(cT, cT.n)]]
        tbl_free = [[(cT, cT.n)], [(cT, cT.n)]]
        svx_free = [[(cV, cV.n)], [(cV, cV.n)]]
        svy_rd = [(cV, cV.n)]

    pr.wait("scalar", (cSTs[0], cSTs[0].n), (cSTs[1], cSTs[1].n))
    pr.wait("gpsimd", (cDB, cDB.n))


_NC_CACHE = {}


def _get_nc(debug=False):
    key = bool(debug)
    if key not in _NC_CACHE:
        _NC_CACHE[key] = build_nc(debug=debug)
    return _NC_CACHE[key]


def make_in_maps(L0_attn_x, L0_attn_wq, L0_attn_wk, L0_attn_wv, L0_attn_wo,
                 L0_ffn_gate_w, L0_ffn_gate_b, L0_ffn_w1, L0_ffn_b1,
                 L0_ffn_w2, L0_ffn_b2):
    f32 = np.float32
    x = np.ascontiguousarray(np.asarray(L0_attn_x, dtype=f32))
    wq = np.asarray(L0_attn_wq, dtype=f32)
    wk = np.asarray(L0_attn_wk, dtype=f32)
    wv = np.asarray(L0_attn_wv, dtype=f32)
    wo = np.asarray(L0_attn_wo, dtype=f32)
    gate_w = np.asarray(L0_ffn_gate_w, dtype=f32)
    gate_b = np.asarray(L0_ffn_gate_b, dtype=f32)
    w1 = np.asarray(L0_ffn_w1, dtype=f32)
    b1 = np.asarray(L0_ffn_b1, dtype=f32)
    w2 = np.asarray(L0_ffn_w2, dtype=f32)
    b2 = np.asarray(L0_ffn_b2, dtype=f32)

    perm = np.concatenate([np.arange(0, D, 2), np.arange(1, D, 2)])
    wqp = np.ascontiguousarray(wq.T[:, perm])
    wkp = np.ascontiguousarray(wk.T[:, perm])
    wvt = np.ascontiguousarray(wv.T)
    wot = np.ascontiguousarray(wo.T)

    import jax
    import jax.numpy as jnp
    with jax.default_device(jax.devices("cpu")[0]):
        inv = 1.0 / 10000.0 ** (jnp.arange(0, D, 2, dtype=jnp.float32) / D)
        tpos = jnp.arange(S, dtype=jnp.float32)
        ang = jnp.outer(tpos, inv)
        cosT = np.asarray(jnp.cos(ang))
        sinT = np.asarray(jnp.sin(ang))
    cosb = np.ascontiguousarray(cosT.T.reshape(4, P, S))
    sinb = np.ascontiguousarray(sinT.T.reshape(4, P, S))
    cosq = np.ascontiguousarray(cosb / 32.0)
    sinq = np.ascontiguousarray(sinb / 32.0)

    gwl = np.ascontiguousarray(
        gate_w.reshape(DC, P, E).transpose(1, 0, 2).reshape(P, DC * E))
    gbb = np.ascontiguousarray(np.broadcast_to(gate_b[None, :], (P, E)))
    b1l = np.ascontiguousarray(
        b1.reshape(E, HCN, P).transpose(2, 0, 1).reshape(P, E * HCN))
    ident = np.eye(P, dtype=ml_dtypes.bfloat16)
    sel8 = np.zeros((E, E * P), dtype=ml_dtypes.bfloat16)
    for e in range(E):
        sel8[e, e * P:(e + 1) * P] = 1.0

    shared = dict(wqp=wqp, wkp=wkp, wvt=wvt, wot=wot, cosb=cosb, sinb=sinb,
                  cosq=cosq, sinq=sinq,
                  gwl=gwl, gbb=gbb, w1l=np.ascontiguousarray(w1), b1l=b1l,
                  w2l=np.ascontiguousarray(w2), b2i=np.ascontiguousarray(b2),
                  idin=ident, sel8=sel8)
    in_maps = []
    for c in range(NCORE):
        m = dict(shared)
        m["xin"] = np.ascontiguousarray(x[c * NSEQ:(c + 1) * NSEQ])
        in_maps.append(m)
    return in_maps


def kernel(**inputs):
    in_maps = make_in_maps(**inputs)
    nc = _get_nc(debug=False)
    res = run_bass_kernel_spmd(nc, in_maps, core_ids=list(range(NCORE)))
    out = np.concatenate([r["yout"] for r in res.results], axis=0)
    return out.reshape(16, S, D)


# version 30
# speedup vs baseline: 1.0080x; 1.0080x over previous
import numpy as np
import ml_dtypes

import concourse.bass as bass
import concourse.mybir as mybir
from concourse.bass_utils import run_bass_kernel_spmd

F32 = mybir.dt.float32
BF16 = mybir.dt.bfloat16
AF = mybir.ActivationFunctionType
ALU = mybir.AluOpType
AX = mybir.AxisListType.X

P = 128
S = 2048
D = 1024
DC = 8
TPS = 16
NSEQ = 2
E = 8
H = 512
HCN = 4
NBLK = 2
BT = 8
NCORE = 8

ENGINES = ("sync", "tensor", "vector", "scalar", "gpsimd")


class Ctr:
    __slots__ = ("sem", "n")

    def __init__(self, sem):
        self.sem = sem
        self.n = 0


class Prog:
    def __init__(self):
        self.ops = {e: [] for e in ENGINES}
        self.cum = set()

    def op(self, eng, meth, *args, waits=(), inc=None, amt=1, **kw):
        waits = [(c, c.n if c in self.cum else v) for (c, v) in waits]
        for (c, v) in waits:
            assert v <= c.n, f"wait targets future value {v} > {c.n} ({meth})"
        pair = None
        if inc is not None:
            inc.n += amt
            pair = (inc.sem, amt)
        wl = tuple((c.sem, int(v)) for c, v in waits if v > 0)

        def f(e, meth=meth, args=args, kw=kw, wl=wl, pair=pair):
            for sm, v in wl:
                e.wait_ge(sm, v)
            ins = getattr(e, meth)(*args, **kw)
            if pair is not None:
                ins.then_inc(*pair)

        self.ops[eng].append(f)
        return inc.n if inc is not None else None

    def wait(self, eng, *waits):
        for (c, v) in waits:
            assert v <= c.n
        wl = tuple((c.sem, int(v)) for c, v in waits if v > 0)

        def f(e, wl=wl):
            for sm, v in wl:
                e.wait_ge(sm, v)

        self.ops[eng].append(f)


def build_nc(debug=False):
    nc = bass.Bass("TRN2", target_bir_lowering=False)

    dr = {}
    dr["xin"] = nc.dram_tensor("xin", [NSEQ, S, D], F32, kind="ExternalInput").ap()
    dr["wqp"] = nc.dram_tensor("wqp", [D, D], F32, kind="ExternalInput").ap()
    dr["wkp"] = nc.dram_tensor("wkp", [D, D], F32, kind="ExternalInput").ap()
    dr["wvt"] = nc.dram_tensor("wvt", [D, D], F32, kind="ExternalInput").ap()
    dr["wot"] = nc.dram_tensor("wot", [D, D], F32, kind="ExternalInput").ap()
    dr["cosb"] = nc.dram_tensor("cosb", [4, P, S], F32, kind="ExternalInput").ap()
    dr["sinb"] = nc.dram_tensor("sinb", [4, P, S], F32, kind="ExternalInput").ap()
    dr["cosq"] = nc.dram_tensor("cosq", [4, P, S], F32, kind="ExternalInput").ap()
    dr["sinq"] = nc.dram_tensor("sinq", [4, P, S], F32, kind="ExternalInput").ap()
    dr["gwl"] = nc.dram_tensor("gwl", [P, DC * E], F32, kind="ExternalInput").ap()
    dr["gbb"] = nc.dram_tensor("gbb", [P, E], F32, kind="ExternalInput").ap()
    dr["w1l"] = nc.dram_tensor("w1l", [E, D, H], F32, kind="ExternalInput").ap()
    dr["b1l"] = nc.dram_tensor("b1l", [P, E * HCN], F32, kind="ExternalInput").ap()
    dr["w2l"] = nc.dram_tensor("w2l", [E, H, D], F32, kind="ExternalInput").ap()
    dr["b2i"] = nc.dram_tensor("b2i", [E, D], F32, kind="ExternalInput").ap()
    dr["idin"] = nc.dram_tensor("idin", [P, P], BF16, kind="ExternalInput").ap()
    dr["sel8"] = nc.dram_tensor("sel8", [E, E * P], BF16, kind="ExternalInput").ap()
    dr["yout"] = nc.dram_tensor("yout", [NSEQ, S, D], F32, kind="ExternalOutput").ap()
    dr["xres"] = nc.dram_tensor("xres", [NSEQ, S, D], F32, kind="Internal").ap()
    dr["w1b"] = nc.dram_tensor("w1b", [E, P, 4096], BF16, kind="Internal").ap()
    dr["w2b"] = nc.dram_tensor("w2b", [E, P, 4096], BF16, kind="Internal").ap()
    dr["qkb"] = nc.dram_tensor("qkb", [4, P, 8192], BF16, kind="Internal").ap()
    dr["cbx"] = nc.dram_tensor("cbx", [4, P, S], BF16, kind="Internal").ap()
    dr["sbx"] = nc.dram_tensor("sbx", [4, P, S], BF16, kind="Internal").ap()
    dr["cqx"] = nc.dram_tensor("cqx", [4, P, S], BF16, kind="Internal").ap()
    dr["sqx"] = nc.dram_tensor("sqx", [4, P, S], BF16, kind="Internal").ap()

    dbg = {}
    if debug:
        for nm, shp, dt in [
            ("dbg_lnfm", [P, DC * S], BF16),
            ("dbg_q", [P, DC * S], BF16),
            ("dbg_k", [P, DC * S], BF16),
            ("dbg_v", [P, TPS * D], BF16),
            ("dbg_lnb", [P, DC * 1024], BF16),
            ("dbg_gfm", [E, S], BF16),
            ("dbg_gh", [P, 16384], BF16),
        ]:
            dbg[nm] = nc.dram_tensor(nm, shp, dt, kind="Internal").ap()

    import contextlib

    stack = contextlib.ExitStack()
    sbf = lambda nm, shp, dt: stack.enter_context(nc.sbuf_tensor(nm, shp, dt))
    psf = lambda nm, shp, dt: stack.enter_context(nc.psum_tensor(nm, shp, dt))
    semf = lambda nm: Ctr(stack.enter_context(nc.semaphore(nm)))

    with stack:
        sbuf = dict(
            LNFM=sbf("LNFM", [P, 16384], BF16),
            QFM=sbf("QFM", [P, 16384], BF16),
            KFM=sbf("KFM", [P, 16384], BF16),
            VTM=sbf("VTM", [P, 16384], BF16),
            WBUF=sbf("WBUF", [P, 16384], BF16),
            STAGE=sbf("STAGE", [P, 2048], F32),
            TBL=sbf("TBL", [P, 4096], BF16),
            XI=sbf("XI", [P, 2048], F32),
            XO=sbf("XO", [P, 2048], F32),
            LNT=sbf("LNT", [P, 2048], BF16),
            AB=sbf("AB", [P, 2048], BF16),
            SV=sbf("SV", [P, 64], F32),
            SVG=sbf("SVG", [P, 384], F32),
            GTB=sbf("GTB", [P, 64], BF16),
            SVX=sbf("SVX", [P, 48], F32),
            SVY=sbf("SVY", [P, 44], F32),
            ONEC=sbf("ONEC", [P, 1], BF16),
            GWSB=sbf("GWSB", [P, DC * E], BF16),
            GBB=sbf("GBB", [P, E], F32),
            B1SB=sbf("B1SB", [P, E * HCN], F32),
            B2SB=sbf("B2SB", [E, D], BF16),
            SEL8=sbf("SEL8", [E, E * P], BF16),
            IDT=sbf("IDT", [P, P], BF16),
            PQ0a=psf("PQ0a", [P, 512], F32),
            PQ0b=psf("PQ0b", [P, 512], F32),
            PQ1a=psf("PQ1a", [P, 512], F32),
            PQ1b=psf("PQ1b", [P, 512], F32),
            PPo=psf("PPo", [P, 1024], F32),
            PPt=psf("PPt", [P, 1024], BF16),
            PPt2=psf("PPt2", [P, 1024], BF16),
        )
        cs = dict(cW=semf("cW"), cW1=semf("cW1"), cWD=semf("cWD"), cWD2=semf("cWD2"),
                  cWT=semf("cWT"), cWT2=semf("cWT2"), cPL=semf("cPL"), cPL2=semf("cPL2"),
                  cTB=semf("cTB"), cTB2=semf("cTB2"),
                  cX0=semf("cX0"), cX1=semf("cX1"),
                  cST0=semf("cST0"), cST1=semf("cST1"), cDB=semf("cDB"), cC=semf("cC"),
                  cT=semf("cT"), cV=semf("cV"), cS=semf("cS"), cG=semf("cG"))

        pr = Prog()
        pr.cum.add(cs["cDB"])
        build_body(pr, dbg, dr, sbuf, cs)

        with nc.Block() as block:
            for eng in ENGINES:
                fns = pr.ops[eng]
                getattr(block, eng)(lambda e, fns=fns: [f(e) for f in fns])

    return nc


def build_body(pr, dbg, dr, sbuf, cs):
    xin, yout, xres = dr["xin"], dr["yout"], dr["xres"]
    LNFM, QFM, KFM, VTM, WBUF = (sbuf[k] for k in ("LNFM", "QFM", "KFM", "VTM", "WBUF"))
    STAGE, TBL, XI, XO, LNT, AB = (sbuf[k] for k in ("STAGE", "TBL", "XI", "XO", "LNT", "AB"))
    SV, SVG, GTB, GWSB, GBB = (sbuf[k] for k in ("SV", "SVG", "GTB", "GWSB", "GBB"))
    B1SB, B2SB, SEL8, IDT = (sbuf[k] for k in ("B1SB", "B2SB", "SEL8", "IDT"))
    ONEC = sbuf["ONEC"]
    SVX = sbuf["SVX"]
    SVY = sbuf["SVY"]
    PQ0a, PQ0b, PQ1a, PQ1b, PPo, PPt, PPt2 = (
        sbuf[k] for k in ("PQ0a", "PQ0b", "PQ1a", "PQ1b", "PPo", "PPt", "PPt2"))
    PQBK = [PQ0a, PQ0b, PQ1a, PQ1b]
    cT, cV, cS, cG = (cs[k] for k in ("cT", "cV", "cS", "cG"))
    cWD = cs["cWD"]
    cWDs = [cs["cWD"], cs["cWD2"]]
    cWTs = [cs["cWT"], cs["cWT2"]]
    cPLs = [cs["cPL"], cs["cPL2"]]
    cTBs = [cs["cTB"], cs["cTB2"]]
    wd_i = [0]
    pl_i = [0]
    cWs = [cs["cW"], cs["cW1"]]
    cXs = [cs["cX0"], cs["cX1"]]
    cSTs = [cs["cST0"], cs["cST1"]]
    cDB = cs["cDB"]
    cC = cs["cC"]

    pr.op("sync", "dma_start", IDT[:], dr["idin"][:], inc=cC, amt=16)
    gww = pr.op("gpsimd", "dma_start", GWSB[:], dr["gwl"][:], waits=[(cWD, cWD.n)], inc=cWD, amt=16)
    pr.op("sync", "dma_start", GBB[:], dr["gbb"][:], inc=cC, amt=16)
    pr.op("sync", "dma_start", B1SB[:], dr["b1l"][:], inc=cC, amt=16)
    pr.op("sync", "dma_start", SEL8[:], dr["sel8"][:], inc=cC, amt=16)
    pr.op("vector", "memset", SV[:, 63:64], 1e-5, inc=cV)
    onec_rdy = pr.op("vector", "memset", ONEC[:], 1.0, inc=cV)
    b2w = pr.op("gpsimd", "dma_start", B2SB[:], dr["b2i"][:], waits=[(cWD, cWD.n)], inc=cWD, amt=16)
    b2c = (cWD, b2w)
    stage_free = [[], []]
    CONSTW = 64

    lnfm_free, qfm_free, kfm_free, vtm_free, ab_free = [], [], [], [], []
    wslot_free = [[], []]
    xi_free = [[], []]
    xo_free = [[], []]
    tbl_free = [[], []]
    lnt_free = [[], []]
    ppt_free = [[], []]
    pqf = [[], [], [], []]
    ppo_free = []
    sv_free = [[], []]
    svq_free = [[], []]
    svg_free = [[] for _ in range(8)]
    xres_st = [None] * (NSEQ * TPS)
    svx_free = [[], []]
    svx_r = [None] * TPS
    svy_free = [[], []]
    svy_rd = []
    svy_r = [None] * TPS
    gtb_free = []
    stg_i = [0]

    def load_weight_bf16(src3, dst, dst_col, pre=(), eng=None):
        C, ncols = src3.shape[1], src3.shape[2]
        dview = dst[:, dst_col:dst_col + C * ncols].rearrange(
            "p (c n) -> p c n", n=ncols)
        sem = cWDs[wd_i[0] % 2]
        wd_i[0] += 1
        w = pr.op("gpsimd", "dma_start", dview, src3[:, :, :],
                  waits=[(sem, sem.n)] + list(pre), inc=sem, amt=16)
        return (sem, w)

    def precast(dstap, srcap, chain=None):
        if chain is None:
            chain = wd_i[0] % 2
            wd_i[0] += 1
        sem = cWDs[chain]
        w = pr.op("gpsimd", "dma_start", dstap, srcap,
                  waits=[(sem, sem.n)], inc=sem, amt=16)
        return (sem, w)

    def load_plain(dst, dst_col, width, srcap, pre=(), eng="sync"):
        sem = cPLs[pl_i[0] % 2]
        pl_i[0] += 1
        w = pr.op(eng, "dma_start", dst[:, dst_col:dst_col + width], srcap,
                  waits=[(sem, sem.n)] + list(pre), inc=sem, amt=16)
        return (sem, w)

    def prefetch_ln1_stats(s, tiles=None, eng="sync"):
        nonlocal svy_rd
        for t in (range(TPS) if tiles is None else tiles):
            k = stg_i[0] % 2
            stg_i[0] += 1
            par = t % 2
            cb = 32 + 6 * par
            c_ss, c_s, c_b, c_t = cb, cb + 1, cb + 2, cb + 3
            w = pr.op(eng, "dma_start", STAGE[:, k * 1024:(k + 1) * 1024],
                      xin[s, t * P:(t + 1) * P, :],
                      waits=list(stage_free[k]), inc=cWs[k], amt=16)
            sm = pr.op("vector", "reduce_sum", SVY[:, c_s:c_s + 1],
                       STAGE[:, k * 1024:(k + 1) * 1024], AX,
                       waits=[(cWs[k], w)] + list(svy_free[par]) + svy_rd, inc=cV)
            sq = pr.op("scalar", "activation", STAGE[:, k * 1024:(k + 1) * 1024],
                       STAGE[:, k * 1024:(k + 1) * 1024], AF.Square,
                       waits=[(cV, sm)] + list(svy_free[par]) + svy_rd, inc=cS,
                       accum_out=SVY[:, c_ss:c_ss + 1])
            svy_rd = []
            stage_free[k] = [(cS, sq)]
            b_ = pr.op("vector", "tensor_scalar", SVY[:, c_b:c_b + 1],
                       SVY[:, c_ss:c_ss + 1], 1.0 / D, 1e-5, ALU.mult,
                       ALU.add, waits=[(cS, sq)], inc=cV)
            m_ = pr.op("scalar", "activation", SVY[:, t:t + 1],
                       SVY[:, c_s:c_s + 1], AF.Copy, scale=1.0 / D, inc=cS)
            mm_ = pr.op("scalar", "activation", SVY[:, c_t:c_t + 1],
                        SVY[:, c_s:c_s + 1], AF.Square, scale=1.0 / D, inc=cS)
            l_ = pr.op("scalar", "activation", SVY[:, c_t:c_t + 1],
                       SVY[:, c_t:c_t + 1], AF.Ln, scale=-1.0,
                       bias=SVY[:, c_b:c_b + 1],
                       waits=[(cS, mm_), (cV, b_)], inc=cS)
            r_ = pr.op("scalar", "activation", SVY[:, TPS + t:TPS + t + 1],
                       SVY[:, c_t:c_t + 1], AF.Exp, scale=-0.5,
                       waits=[(cS, l_)], inc=cS)
            svy_free[par] = [(cS, r_)]
            svy_r[t] = (cS, r_)

    def ln_tile(sl, evac_view, psl, extra_evac_waits):
        b0 = 16 + 8 * sl
        c_s, c_ss, c_b, c_m, c_mm, c_ln, c_r = range(b0, b0 + 7)
        xiw = xi_free[sl][0]
        sq = pr.op("scalar", "activation", XO[:, sl * 1024:(sl + 1) * 1024],
                   XI[:, sl * 1024:(sl + 1) * 1024], AF.Square,
                   waits=[xiw] + list(xo_free[sl]) + list(sv_free[sl]),
                   inc=cS, accum_out=SV[:, c_ss:c_ss + 1])
        xo_free[sl] = [(cS, sq)]
        sm = pr.op("vector", "reduce_sum", SV[:, c_s:c_s + 1],
                   XI[:, sl * 1024:(sl + 1) * 1024], AX,
                   waits=[xiw] + list(sv_free[sl]), inc=cV)
        bq = pr.op("vector", "tensor_scalar", SV[:, c_b:c_b + 1],
                   SV[:, c_ss:c_ss + 1], 1.0 / D, 1e-5, ALU.mult, ALU.add,
                   waits=[(cS, sq), (cV, sm)], inc=cV)
        m1_ = pr.op("scalar", "activation", SV[:, c_m:c_m + 1],
                    SV[:, c_s:c_s + 1], AF.Copy, scale=1.0 / D,
                    waits=[(cV, bq)], inc=cS)
        mm_ = pr.op("scalar", "activation", SV[:, c_mm:c_mm + 1],
                    SV[:, c_s:c_s + 1], AF.Square, scale=1.0 / D, inc=cS)
        l1 = pr.op("scalar", "activation", SV[:, c_ln:c_ln + 1],
                   SV[:, c_mm:c_mm + 1], AF.Ln, bias=SV[:, c_b:c_b + 1],
                   scale=-1.0, waits=[(cS, mm_)], inc=cS)
        r1 = pr.op("scalar", "activation", SV[:, c_r:c_r + 1],
                   SV[:, c_ln:c_ln + 1], AF.Exp, scale=-0.5,
                   waits=[(cS, l1)], inc=cS)
        sv_free[sl] = [(cS, r1)]
        ap = pr.op("vector", "tensor_scalar", LNT[:, sl * 1024:(sl + 1) * 1024],
                   XI[:, sl * 1024:(sl + 1) * 1024], SV[:, c_m:c_m + 1],
                   SV[:, c_r:c_r + 1], ALU.subtract, ALU.mult,
                   waits=[(cS, r1)] + list(lnt_free[sl]), inc=cV)
        xi_free[sl] = [(cV, ap)]
        pi = 0 if psl is PPt else 1
        tp = None
        for dc_ in range(DC):
            tp = pr.op("tensor", "transpose", psl[:, dc_ * P:(dc_ + 1) * P],
                       LNT[:, sl * 1024 + dc_ * P: sl * 1024 + (dc_ + 1) * P],
                       IDT[:],
                       waits=([(cV, ap), (cC, CONSTW)] + list(ppt_free[pi])) if dc_ == 0 else [],
                       inc=cT if dc_ == DC - 1 else None)
        lnt_free[sl] = [(cT, tp)]
        return (psl, pi, tp, evac_view, list(extra_evac_waits))

    def ln_evac(info):
        psl, pi, tp, evac_view, extra = info
        pv = psl[:, :].rearrange("p (c w) -> p c w", c=DC)
        ev = pr.op("scalar", "copy", evac_view, pv[:, :, :],
                   waits=[(cT, tp)] + extra, inc=cS)
        ppt_free[pi] = [(cS, ev)]
        return ev

    pc = {}
    for s in range(NSEQ):
        if s == 0:
            prefetch_ln1_stats(0, tiles=range(0, 4))
        wv_src = dr["wvt"].rearrange("(c p) (j n) -> p c j n", p=P, j=2)
        wv_dst = WBUF[:, 0:8192].rearrange("p (c j n) -> p c j n", c=DC, j=2)
        wv_rdy_h = []
        for j in range(2):
            sem = cWDs[wd_i[0] % 2]
            wd_i[0] += 1
            w = pr.op("gpsimd", "dma_start",
                      wv_dst[:, :, j, :], wv_src[:, :, j, :],
                      waits=[(sem, sem.n)] + (list(wslot_free[0]) if j == 0 else []),
                      inc=sem, amt=16)
            wv_rdy_h.append((sem, w))
        wk_rdy = load_weight_bf16(dr["wkp"].rearrange("(c p) n -> p c n", p=P),
                                  WBUF, 8192, pre=wslot_free[1])
        wslot_free[0] = []
        wslot_free[1] = []
        if s == 0:
            prefetch_ln1_stats(0, tiles=range(4, TPS))

        lv = LNFM[:, :].rearrange("p (c s2) -> p c s2", c=DC)
        ln1_ev = [0] * TPS
        v_done = 0
        pend = None

        def emit_v(t, ev):
            nonlocal v_done, vtm_free, ppo_free
            use_pp0 = (t % 2 == 1) and t <= 8
            fr = (pqf[0] + pqf[1]) if use_pp0 else ppo_free
            mm = None
            for dc_ in range(DC):
                for n2 in range(2):
                    pvs = PQBK[n2][:, 0:512] if use_pp0 else PPo[:, n2 * 512:(n2 + 1) * 512]
                    w_ = []
                    if dc_ == 0:
                        w_ = [wv_rdy_h[n2]]
                        if n2 == 0:
                            w_ = w_ + [(cS, ev)] + list(fr)
                    mm = pr.op(
                        "tensor", "matmul", pvs,
                        LNFM[:, dc_ * 2048 + t * P: dc_ * 2048 + (t + 1) * P],
                        WBUF[:, dc_ * 1024 + n2 * 512: dc_ * 1024 + (n2 + 1) * 512],
                        start=(dc_ == 0), stop=(dc_ == DC - 1),
                        waits=w_,
                        inc=cT if (dc_ == DC - 1 and n2 == 1) else None)
            if use_pp0:
                pr.op("scalar", "copy", VTM[:, t * 1024:t * 1024 + 512],
                      PQ0a[:, 0:512],
                      waits=[(cT, mm)] + (list(vtm_free) if t == 0 else []),
                      inc=cS)
                v_done = pr.op("scalar", "copy",
                               VTM[:, t * 1024 + 512:(t + 1) * 1024],
                               PQ0b[:, 0:512], inc=cS)
            else:
                v_done = pr.op("scalar", "copy", VTM[:, t * 1024:(t + 1) * 1024],
                               PPo[:, :],
                               waits=[(cT, mm)] + (list(vtm_free) if t == 0 else []),
                               inc=cS)
            if t == 0:
                vtm_free = []
            if use_pp0:
                pqf[0] = [(cS, v_done)]
                pqf[1] = [(cS, v_done)]
            else:
                ppo_free = [(cS, v_done)]

        def emit_pair(FM, base_w, wrdy, c, h, tbl_evac=False):
            nonlocal ab_free
            sl = 0 if tbl_evac else (c * 2 + h) % 2
            if s == 0:
                cosrc = dr["cosb"] if FM is KFM else dr["cosq"]
                sisrc = dr["sinb"] if FM is KFM else dr["sinq"]
                teng = "gpsimd"
                tch = cWTs
                pcw = []
            else:
                cosrc = dr["cbx"] if FM is KFM else dr["cqx"]
                sisrc = dr["sbx"] if FM is KFM else dr["sqx"]
                teng = "sync"
                tch = cTBs
                pcw = [pc["tb"]]
            tcc_s = pr.op(teng, "dma_start", TBL[:, sl * 2048: sl * 2048 + 1024],
                          cosrc[c, :, h * 1024:(h + 1) * 1024],
                          waits=[(tch[0], tch[0].n)] + list(tbl_free[sl]) + pcw,
                          inc=tch[0], amt=16)
            tcc = (tch[0], tcc_s)
            tcs_s = pr.op(teng, "dma_start",
                          TBL[:, sl * 2048 + 1024: sl * 2048 + 2048],
                          sisrc[c, :, h * 1024:(h + 1) * 1024],
                          waits=[(tch[1], tch[1].n)] + list(tbl_free[sl]) + pcw,
                          inc=tch[1], amt=16)
            tcs = (tch[1], tcs_s)
            cosl = TBL[:, sl * 2048: sl * 2048 + 1024]
            sinl = TBL[:, sl * 2048 + 1024: sl * 2048 + 2048]
            if tbl_evac:
                eq0, eq1 = TBL[:, 2048:3072], TBL[:, 3072:4096]
                eqf = tbl_free[1]
            else:
                eq0, eq1 = LNT[:, 0:1024], LNT[:, 1024:2048]

            mgd = [None] * 4
            for pi, part in enumerate((c, c + 4)):
                for dc_ in range(DC):
                    for n2 in range(2):
                        first = dc_ == 0 and n2 == 0
                        mm = pr.op(
                            "tensor", "matmul",
                            PQBK[pi * 2 + n2][:, 0:512],
                            WBUF[:, base_w + dc_ * 1024 + part * P: base_w + dc_ * 1024 + (part + 1) * P],
                            LNFM[:, dc_ * 2048 + h * 1024 + n2 * 512: dc_ * 2048 + h * 1024 + (n2 + 1) * 512],
                            start=(dc_ == 0), stop=(dc_ == DC - 1),
                            waits=((list(wrdy) + [(cS, ln1_ev[8 * h + 7])] if (first and pi == 0) else []) + (pqf[pi * 2 + n2] if dc_ == 0 else [])),
                            inc=cT if dc_ == DC - 1 else None)
                        if dc_ == DC - 1:
                            mgd[pi * 2 + n2] = mm
            e1 = pr.op("scalar", "copy", eq0[:, 0:512], PQ0a[:, 0:512],
                       waits=[(cT, mgd[0])] + (list(eqf) if tbl_evac else list(lnt_free[0])), inc=cS)
            e1b = pr.op("scalar", "copy", eq0[:, 512:1024], PQ0b[:, 0:512],
                        waits=[(cT, mgd[1])], inc=cS)
            e2a = pr.op("scalar", "copy", eq1[:, 0:512], PQ1a[:, 0:512],
                        waits=[(cT, mgd[2])] + ([] if tbl_evac else list(lnt_free[1])), inc=cS)
            e2 = pr.op("scalar", "copy", eq1[:, 512:1024], PQ1b[:, 0:512],
                       waits=[(cT, mgd[3])], inc=cS)
            pqf[0] = [(cS, e1)]
            pqf[1] = [(cS, e1b)]
            pqf[2] = [(cS, e2a)]
            pqf[3] = [(cS, e2)]
            qr = eq0
            qi = eq1
            oc = FM[:, c * 2048 + h * 1024: c * 2048 + (h + 1) * 1024]
            oc4 = FM[:, (c + 4) * 2048 + h * 1024: (c + 4) * 2048 + (h + 1) * 1024]
            t1 = pr.op("vector", "tensor_tensor", oc, qr, cosl, ALU.mult,
                       waits=[(cS, e2), tcc, tcs] + list(ab_free), inc=cV)
            t2 = pr.op("vector", "tensor_tensor", AB[:, 0:1024], qi,
                       sinl, ALU.mult, inc=cV)
            t3 = pr.op("vector", "tensor_tensor", oc, oc, AB[:, 0:1024],
                       ALU.subtract, waits=[(cV, t2)], inc=cV)
            t4 = pr.op("vector", "tensor_tensor", oc4, qr, sinl,
                       ALU.mult, inc=cV)
            t5 = pr.op("vector", "tensor_tensor", AB[:, 1024:2048], qi,
                       cosl, ALU.mult, inc=cV)
            t6 = pr.op("vector", "tensor_tensor", oc4, oc4,
                       AB[:, 1024:2048], ALU.add, waits=[(cV, t5)],
                       inc=cV)
            if tbl_evac:
                tbl_free[1] = [(cV, t6)]
            else:
                lnt_free[0] = [(cV, t6)]
                lnt_free[1] = [(cV, t6)]
            ab_free = [(cV, t6)]
            tbl_free[sl] = [(cV, t6)] if not tbl_evac else (tbl_free[sl] + [(cV, t6)])


        for t in range(TPS):
            sl = t % 2
            w = pr.op("sync", "dma_start", XI[:, sl * 1024:(sl + 1) * 1024],
                      xin[s, t * P:(t + 1) * P, :],
                      waits=list(xi_free[sl]), inc=cXs[sl], amt=16)
            xi_free[sl] = [(cXs[sl], w)]
            ap1 = pr.op("vector", "tensor_scalar",
                        LNT[:, sl * 1024:(sl + 1) * 1024],
                        XI[:, sl * 1024:(sl + 1) * 1024],
                        SVY[:, t:t + 1], SVY[:, TPS + t:TPS + t + 1],
                        ALU.subtract, ALU.mult,
                        waits=[(cXs[sl], w), svy_r[t]] + list(lnt_free[sl]),
                        inc=cV)
            xi_free[sl] = [(cV, ap1)]
            psl1 = PPt if sl == 0 else PPt2
            pi1 = 0 if sl == 0 else 1
            tp1 = None
            for dc_ in range(DC):
                tp1 = pr.op("tensor", "transpose",
                            psl1[:, dc_ * P:(dc_ + 1) * P],
                            LNT[:, sl * 1024 + dc_ * P: sl * 1024 + (dc_ + 1) * P],
                            IDT[:],
                            waits=([(cV, ap1), (cC, CONSTW)] + list(ppt_free[pi1])) if dc_ == 0 else [],
                            inc=cT if dc_ == DC - 1 else None)
            lnt_free[sl] = [(cT, tp1)]
            info = (psl1, pi1, tp1, lv[:, :, t * P:(t + 1) * P],
                    list(lnfm_free) if t == 0 else [])
            if t == 0:
                lnfm_free = []
            if pend is not None:
                ln1_ev[t - 1] = ln_evac(pend)
                if t - 1 >= 8 and (t - 1) % 2 == 0:
                    emit_pair(KFM, 8192, [wk_rdy], (t - 1 - 8) // 2, 0, tbl_evac=True)
                emit_v(t - 1, ln1_ev[t - 1])
            pend = info
        ln1_ev[TPS - 1] = ln_evac(pend)
        emit_v(TPS - 1, ln1_ev[TPS - 1])

        if dbg and s == 0:
            pr.op("gpsimd", "dma_start", dbg["dbg_lnfm"][:], LNFM[:, :],
                  waits=[(cS, ln1_ev[-1])], inc=cDB, amt=16)
            lnfm_free = lnfm_free + [(cDB, cDB.n)]
            pr.op("gpsimd", "dma_start", dbg["dbg_v"][:], VTM[:, :],
                  waits=[(cS, v_done)], inc=cDB, amt=16)
            vtm_free = vtm_free + [(cDB, cDB.n)]

        wq_src = dr["wqp"].rearrange("(c p) (g q) -> p c g q", p=P, g=8)
        wq_dst = WBUF[:, 0:8192].rearrange("p (c g q) -> p c g q", c=DC, g=8)
        wq_rdy_g = []
        for g in (0, 4, 1, 5, 2, 6, 3, 7):
            sem = cWDs[wd_i[0] % 2]
            wd_i[0] += 1
            w = pr.op("gpsimd", "dma_start",
                      wq_dst[:, :, g, :], wq_src[:, :, g, :],
                      waits=[(sem, sem.n)] + ([(cT, cT.n)] if g == 0 else []),
                      inc=sem, amt=16)
            wq_rdy_g.append((g, (sem, w)))
        wq_rdy_g.sort()
        wq_rdy_c = [[wq_rdy_g[c][1], wq_rdy_g[c + 4][1]] for c in range(4)]

        for c in range(4):
            emit_pair(KFM, 8192, [wk_rdy], c, 1)
        wo_rdy = load_weight_bf16(
            dr["wot"].rearrange("(c p) n -> p c n", p=P),
            WBUF, 8192, pre=[(cT, cT.n)])
        for h in range(2):
            for c in range(4):
                emit_pair(QFM, 0, wq_rdy_c[c], c, h)
        q_done = cV.n
        ab_free = [(cV, q_done)]

        if dbg and s == 0:
            pr.op("gpsimd", "dma_start", dbg["dbg_q"][:], QFM[:, :],
                  waits=[(cV, q_done)], inc=cDB, amt=16)
            qfm_free = qfm_free + [(cDB, cDB.n)]
            pr.op("gpsimd", "dma_start", dbg["dbg_k"][:], KFM[:, :],
                  waits=[(cV, q_done)], inc=cDB, amt=16)
            kfm_free = kfm_free + [(cDB, cDB.n)]

        qk_mm_done = cT.n
        w2_rdy = [(cS, 0)] * E
        if s == 0:
            pc["w1"] = []
            for e in range(E):
                src = dr["w1l"][e].rearrange("(c p) n -> p c n", p=P)
                dst = dr["w1b"][e].rearrange("p (c n) -> p c n", n=H)
                pc["w1"].append(precast(dst, src))
            for e in range(6):
                dst = LNFM if e < 4 else WBUF
                col = (e % 4) * 4096
                src = dr["w2l"][e].rearrange("(c p) n -> p c n", p=P)
                w2_rdy[e] = load_weight_bf16(
                    src, dst, col,
                    pre=([(cT, qk_mm_done)] + (list(lnfm_free) if e == 0 else [])))
        lnfm_free = []
        wslot_free = [[], []]

        def w1slot(e):
            wsl = e % 3
            if wsl == 0:
                return TBL, 0
            return VTM, 8192 + (wsl - 1) * 4096

        def load_w1(e, pre):
            dst, col = w1slot(e)
            return load_plain(dst, col, 4096, dr["w1b"][e],
                              pre=list(pre) + [pc["w1"][e]])

        w1pre = {}
        tblw_saved = tbl_free[0] + tbl_free[1]
        tbl_free = [[], []]

        def emit_wo_transp(t):
            sl = t % 2
            w = pr.op("sync", "dma_start", XI[:, sl * 1024:(sl + 1) * 1024],
                      xin[s, t * P:(t + 1) * P, :],
                      waits=list(xi_free[sl]), inc=cXs[sl], amt=16)
            tp = None
            for dc_ in range(DC):
                tp = pr.op("tensor", "transpose", PPt[:, dc_ * P:(dc_ + 1) * P],
                           LNT[:, dc_ * P:(dc_ + 1) * P], IDT[:],
                           waits=(list(ppt_free[0]) + list(avev[t])) if dc_ == 0 else [],
                           inc=cT if dc_ == DC - 1 else None)
            lnt_free[0] = [(cT, tp)]
            ev = pr.op("vector", "tensor_copy", LNT[:, 1024:2048], PPt[:, :],
                       waits=[(cT, tp)] + list(lnt_free[1]), inc=cV)
            ppt_free[0] = [(cV, ev)]
            return (sl, w, ev)

        def emit_ln2stats(t):
            sl = t % 2
            ad = adres[t]
            par = t % 2
            cb = 32 + 6 * par
            c2_ss, c2_s, c2_b, c2_t = cb, cb + 1, cb + 2, cb + 3
            sq2 = pr.op("scalar", "activation", XI[:, sl * 1024:(sl + 1) * 1024],
                        XO[:, sl * 1024:(sl + 1) * 1024], AF.Square,
                        waits=[(cV, ad)] + list(svx_free[par]), inc=cS,
                        accum_out=SVX[:, c2_ss:c2_ss + 1])
            xi_free[sl] = [(cV, ad), (cS, sq2)]
            sm2 = pr.op("vector", "reduce_sum", SVX[:, c2_s:c2_s + 1],
                        XO[:, sl * 1024:(sl + 1) * 1024], AX,
                        waits=[(cV, ad)] + list(svx_free[par]), inc=cV)
            b2_ = pr.op("vector", "tensor_scalar", SVX[:, c2_b:c2_b + 1],
                        SVX[:, c2_ss:c2_ss + 1], 1.0 / D, 1e-5, ALU.mult,
                        ALU.add, waits=[(cS, sq2), (cV, sm2)], inc=cV)
            m2_ = pr.op("scalar", "activation", SVX[:, t:t + 1],
                        SVX[:, c2_s:c2_s + 1], AF.Copy, scale=1.0 / D,
                        waits=[(cV, sm2)] + list(svx_free[par]), inc=cS)
            mm2 = pr.op("scalar", "activation", SVX[:, c2_t:c2_t + 1],
                        SVX[:, c2_s:c2_s + 1], AF.Square, scale=1.0 / D,
                        inc=cS)
            ln2_ = pr.op("scalar", "activation", SVX[:, c2_t:c2_t + 1],
                         SVX[:, c2_t:c2_t + 1], AF.Ln, scale=-1.0,
                         bias=SVX[:, c2_b:c2_b + 1],
                         waits=[(cS, mm2), (cV, b2_)], inc=cS)
            r2_ = pr.op("scalar", "activation", SVX[:, TPS + t:TPS + t + 1],
                        SVX[:, c2_t:c2_t + 1], AF.Exp, scale=-0.5,
                        waits=[(cS, ln2_)], inc=cS)
            svx_free[par] = [(cS, r2_)]
            svx_r[t] = (cS, r2_)

        def emit_wo_mm(t, st1):
            sl, w, ev = st1
            mm = None
            for dc_ in range(DC):
                for n2 in range(2):
                    first = dc_ == 0 and n2 == 0
                    mm = pr.op(
                        "tensor", "matmul", PPo[:, n2 * 512:(n2 + 1) * 512],
                        LNT[:, 1024 + dc_ * P: 1024 + (dc_ + 1) * P],
                        WBUF[:, 8192 + dc_ * 1024 + n2 * 512: 8192 + dc_ * 1024 + (n2 + 1) * 512],
                        start=(dc_ == 0), stop=(dc_ == DC - 1),
                        waits=([(cV, ev), wo_rdy] + list(ppo_free)) if first else [],
                        inc=cT if (dc_ == DC - 1 and n2 == 1) else None)
            lnt_free[1] = [(cT, mm)]
            return (sl, w, mm)

        def emit_wo_fin(t, st2):
            sl, w, mm = st2
            ad = pr.op("vector", "tensor_tensor", XO[:, sl * 1024:(sl + 1) * 1024],
                       XI[:, sl * 1024:(sl + 1) * 1024], PPo[:, :], ALU.add,
                       waits=[(cT, mm), (cXs[sl], w)] + list(xo_free[sl]), inc=cV)
            st = pr.op("scalar", "dma_start", xres[s, t * P:(t + 1) * P, :],
                       XO[:, sl * 1024:(sl + 1) * 1024], waits=[(cV, ad)],
                       inc=cSTs[sl], amt=16)
            xo_free[sl] = [(cSTs[sl], st)]
            xres_st[s * TPS + t] = (cSTs[sl], st)
            return ad

        avev = [None] * TPS
        adres = [None] * TPS
        for qt in range(TPS):
            if qt == 6:
                w1pre[0] = load_w1(0, pre=tblw_saved)
            if s == 1 and qt in (1, 3, 5, 7, 9, 11):
                e_ = qt // 2
                dst = LNFM if e_ < 4 else WBUF
                w2_rdy[e_] = load_plain(
                    dst, (e_ % 4) * 4096, 4096, dr["w2b"][e_],
                    pre=[(cT, qk_mm_done), pc["w2"][e_]] + (list(lnfm_free) if e_ == 0 else []))
            par = qt % 2
            b0 = 40 + 8 * par
            c_p0, c_p1, c_p2, c_p3, c_nm, c_l, c_rl, c_l1 = range(b0, b0 + 8)
            cps = [c_p0, c_p1, c_p2, c_p3]

            def score_kb(kb):
                mm = None
                for dc_ in range(DC):
                    mm = pr.op(
                        "tensor", "matmul", PQBK[kb][:, 0:512],
                        QFM[:, dc_ * 2048 + qt * P: dc_ * 2048 + (qt + 1) * P],
                        KFM[:, dc_ * 2048 + kb * 512: dc_ * 2048 + (kb + 1) * 512],
                        start=(dc_ == 0), stop=(dc_ == DC - 1),
                        waits=(([(cV, q_done)] if kb == 0 else []) + pqf[kb]) if dc_ == 0 else [],
                        inc=cT if dc_ == DC - 1 else None)
                return mm

            kbdone = [None] * 4
            kbdone[0] = score_kb(0)
            kbdone[1] = score_kb(1)
            st1 = emit_wo_transp(qt - 1) if qt > 0 else None
            kbdone[2] = score_kb(2)
            kbdone[3] = score_kb(3)
            nm = None
            for kb in range(4):
                nmk = pr.op("vector", "reduce_max", SV[:, cps[kb]:cps[kb] + 1],
                            PQBK[kb][:, 0:512], AX, negate=True,
                            waits=[(cT, kbdone[kb])] + (list(svq_free[par]) if kb == 0 else []),
                            inc=cV)
                if kb > 0:
                    acc = cps[0] if kb == 1 else c_nm
                    nm = pr.op("vector", "tensor_tensor", SV[:, c_nm:c_nm + 1],
                               SV[:, acc:acc + 1], SV[:, cps[kb]:cps[kb] + 1],
                               ALU.min, waits=[(cV, nmk)], inc=cV)
            exs = []
            ex_ = None
            for kb in range(4):
                ex_ = pr.op("scalar", "activation", AB[:, kb * 512:(kb + 1) * 512],
                            PQBK[kb][:, 0:512],
                            AF.Exp, bias=SV[:, c_nm:c_nm + 1], scale=1.0,
                            waits=([(cV, nm)] + list(ab_free)) if kb == 0 else [],
                            inc=cS)
                exs.append(ex_)
            rs0 = pr.op("vector", "reduce_sum", SV[:, c_l:c_l + 1],
                        AB[:, 0:1024], AX, waits=[(cS, exs[1])], inc=cV)
            rs1 = pr.op("vector", "reduce_sum", SV[:, c_l1:c_l1 + 1],
                        AB[:, 1024:2048], AX, waits=[(cS, exs[3])], inc=cV)
            lsm = pr.op("vector", "tensor_tensor", SV[:, c_l:c_l + 1],
                        SV[:, c_l:c_l + 1], SV[:, c_l1:c_l1 + 1], ALU.add,
                        waits=[(cV, rs1)], inc=cV)
            rc = pr.op("vector", "reciprocal", SV[:, c_rl:c_rl + 1],
                       SV[:, c_l:c_l + 1], waits=[(cV, lsm)], inc=cV)
            svq_free[par] = [(cV, rc)]
            st2 = emit_wo_mm(qt - 1, st1) if qt > 0 else None
            if qt > 0:
                sl_, w_, mm_ = st2
                ad_prev = pr.op("vector", "tensor_tensor",
                                XO[:, sl_ * 1024:(sl_ + 1) * 1024],
                                XI[:, sl_ * 1024:(sl_ + 1) * 1024], PPo[:, :],
                                ALU.add,
                                waits=[(cT, mm_), (cXs[sl_], w_)] + list(xo_free[sl_]),
                                inc=cV)
                adres[qt - 1] = ad_prev
                ppo_free = [(cV, ad_prev)]
            evls = []
            for g in range(2):
                pt = PPt if g == 0 else PPt2
                tp = None
                for j in range(8):
                    kt = g * 8 + j
                    tp = pr.op("tensor", "transpose", pt[:, j * P:(j + 1) * P],
                               AB[:, kt * P:(kt + 1) * P], IDT[:],
                               waits=([(cS, exs[2 * g + 1])] + list(ppt_free[g])) if j == 0 else [],
                               inc=cT if j == 7 else None)
                evl = pr.op("scalar", "copy", AB[:, g * 1024:(g + 1) * 1024],
                            pt[:, :],
                            waits=[(cT, tp), (cV, rs0 if g == 0 else rs1)],
                            inc=cS)
                evls.append(evl)
                ppt_free[g] = [(cS, evl)]
            if qt > 0:
                sl_, w_, mm_ = st2
                stx = pr.op("scalar", "dma_start",
                            xres[s, (qt - 1) * P:qt * P, :],
                            XO[:, sl_ * 1024:(sl_ + 1) * 1024],
                            waits=[(cV, adres[qt - 1])], inc=cSTs[sl_], amt=16)
                xo_free[sl_] = [(cSTs[sl_], stx)]
                xres_st[s * TPS + qt - 1] = (cSTs[sl_], stx)
            if qt > 1:
                emit_ln2stats(qt - 2)
            pqf[0] = [(cS, exs[0])]
            pqf[1] = [(cS, exs[1])]
            pqf[2] = [(cS, exs[2])]
            pqf[3] = [(cS, exs[3])]
            mm = None
            for n2 in range(2):
                for kt in range(TPS):
                    first = n2 == 0 and kt == 0
                    mm = pr.op(
                        "tensor", "matmul", PPo[:, n2 * 512:(n2 + 1) * 512],
                        AB[:, kt * P:(kt + 1) * P],
                        VTM[:, kt * 1024 + n2 * 512: kt * 1024 + (n2 + 1) * 512],
                        start=(kt == 0), stop=(kt == TPS - 1),
                        waits=([(cS, v_done), (cS, evls[0])] + list(ppo_free)) if first else ([(cS, evls[1])] if (n2 == 0 and kt == 8) else []),
                        inc=cT if (n2 == 1 and kt == TPS - 1) else None)
            ab_free = [(cT, mm)]
            av = pr.op("vector", "tensor_scalar_mul", LNT[:, 0:1024], PPo[:, :],
                       SV[:, c_rl:c_rl + 1],
                       waits=[(cT, mm), (cV, rc)] + list(lnt_free[0]), inc=cV)
            ppo_free = [(cV, av)]
            avev[qt] = [(cV, av)]
        st1 = emit_wo_transp(TPS - 1)
        ad_last = emit_wo_fin(TPS - 1, emit_wo_mm(TPS - 1, st1))
        adres[TPS - 1] = ad_last
        ppo_free = [(cV, ad_last)]
        emit_ln2stats(TPS - 2)
        emit_ln2stats(TPS - 1)
        vtmhi_free = [(cT, cT.n), (cDB, cDB.n)]
        for e_ in (1, 2):
            dst_, col_ = w1slot(e_)
            w1pre[e_] = load_plain(dst_, col_, 4096, dr["w1b"][e_],
                                   pre=list(vtmhi_free) + [pc["w1"][e_]],
                                   eng="scalar")
        for e in (6, 7):
            if s == 0:
                src = dr["w2l"][e].rearrange("(c p) n -> p c n", p=P)
                w2_rdy[e] = load_weight_bf16(src, WBUF, (e - 4) * 4096,
                                             pre=[(cT, cT.n)])
            else:
                w2_rdy[e] = load_plain(WBUF, (e - 4) * 4096, 4096,
                                       dr["w2b"][e],
                                       pre=[(cT, cT.n), pc["w2"][e]])
        if s == 0:
            tb_last = None
            for dstnm, srcnm in (("cbx", "cosb"), ("sbx", "sinb"),
                                 ("cqx", "cosq"), ("sqx", "sinq")):
                tb_last = precast(dr[dstnm][:], dr[srcnm][:], chain=0)
            pc["tb"] = tb_last
            pc["w2"] = []
            for e in range(E):
                src = dr["w2l"][e].rearrange("(c p) n -> p c n", p=P)
                dst = dr["w2b"][e].rearrange("p (c n) -> p c n", n=1024)
                pc["w2"].append(precast(dst, src, chain=0))

        lnb = VTM[:, 0:8192].rearrange("p (c s2) -> p c s2", c=DC)
        vtm_free = [(cT, cT.n), (cDB, cDB.n)]
        w1slot_free = [[], list(vtmhi_free), list(vtmhi_free)]
        for b in range(NBLK):
            ev_last = None

            def emit_gating2(pairs):
                ctx = []
                for bt, ev in pairs:
                    sl = bt
                    gb = 46 * bt
                    ctx.append(dict(
                        bt=bt, ev=ev, sl=sl,
                        LG=SVG[:, gb:gb + 8], L2=SVG[:, gb + 8:gb + 16],
                        M1=SVG[:, gb + 16:gb + 24], M2=SVG[:, gb + 24:gb + 32],
                        GT=SVG[:, gb + 32:gb + 40],
                        sc=list(range(gb + 40, gb + 46))))
                S_ = lambda c: SVG[:, c:c + 1]
                for x in ctx:
                    gm = None
                    for dc_ in range(DC):
                        gm = pr.op("tensor", "matmul", PQ0a[:, 0:8],
                                   lnb[:, dc_, x["bt"] * P:(x["bt"] + 1) * P],
                                   GWSB[:, dc_ * 8:(dc_ + 1) * 8],
                                   start=(dc_ == 0), stop=(dc_ == DC - 1),
                                   waits=([(cS, x["ev"]), (cC, CONSTW), (cWD, gww)] + pqf[0]) if dc_ == 0 else [],
                                   inc=cT if dc_ == DC - 1 else None)
                    lg = pr.op("scalar", "copy", x["LG"], PQ0a[:, 0:8],
                               waits=[(cT, gm)] + list(svg_free[x["sl"]]), inc=cS)
                    pqf[0] = [(cS, lg)]
                    x["lg"] = lg
                for x in ctx:
                    x["g1"] = pr.op("vector", "tensor_tensor", x["LG"], x["LG"],
                                    GBB[:], ALU.add, waits=[(cS, x["lg"])], inc=cV)
                for x in ctx:
                    x["g2"] = pr.op("vector", "reduce_max", S_(x["sc"][0]), x["LG"],
                                    AX, waits=[(cV, x["g1"])], inc=cV)
                for x in ctx:
                    x["g3"] = pr.op("vector", "tensor_single_scalar", x["M1"],
                                    x["LG"], S_(x["sc"][0]), ALU.is_equal,
                                    waits=[(cV, x["g2"])], inc=cV)
                for x in ctx:
                    x["g4"] = pr.op("vector", "scalar_tensor_tensor", x["L2"],
                                    x["M1"], -1e30, x["LG"], ALU.mult, ALU.add,
                                    waits=[(cV, x["g3"])], inc=cV)
                for x in ctx:
                    x["g5"] = pr.op("vector", "reduce_max", S_(x["sc"][1]),
                                    x["L2"], AX, waits=[(cV, x["g4"])], inc=cV)
                for x in ctx:
                    x["g6"] = pr.op("vector", "tensor_single_scalar", x["M2"],
                                    x["L2"], S_(x["sc"][1]), ALU.is_equal,
                                    waits=[(cV, x["g5"])], inc=cV)
                for x in ctx:
                    x["g7"] = pr.op("vector", "tensor_tensor", S_(x["sc"][2]),
                                    S_(x["sc"][1]), S_(x["sc"][0]), ALU.subtract,
                                    waits=[(cV, x["g5"])], inc=cV)
                for x in ctx:
                    x["g8"] = pr.op("scalar", "activation", S_(x["sc"][3]),
                                    S_(x["sc"][2]), AF.Exp,
                                    waits=[(cV, x["g7"])], inc=cS)
                for x in ctx:
                    x["g9"] = pr.op("vector", "tensor_scalar_add", S_(x["sc"][4]),
                                    S_(x["sc"][3]), 1.0, waits=[(cS, x["g8"])],
                                    inc=cV)
                for x in ctx:
                    x["g10"] = pr.op("vector", "reciprocal", S_(x["sc"][5]),
                                     S_(x["sc"][4]), waits=[(cV, x["g9"])], inc=cV)
                for x in ctx:
                    x["g11"] = pr.op("vector", "scalar_tensor_tensor", x["GT"],
                                     x["M2"], S_(x["sc"][3]), x["M1"], ALU.mult,
                                     ALU.add, waits=[(cV, x["g6"]), (cS, x["g8"])],
                                     inc=cV)
                for x in ctx:
                    x["g12"] = pr.op("vector", "tensor_scalar_mul", x["GT"],
                                     x["GT"], S_(x["sc"][5]),
                                     waits=[(cV, x["g10"]), (cV, x["g11"])], inc=cV)
                for x in ctx:
                    g13 = pr.op("vector", "tensor_copy",
                                GTB[:, x["bt"] * 8:(x["bt"] + 1) * 8], x["GT"],
                                waits=[(cV, x["g12"])] + (gtb_free if x["bt"] == 0 else []),
                                inc=cV)
                    svg_free[x["sl"]] = [(cV, g13)]

            pend2 = None
            evs2 = []
            for bt in range(BT):
                t = b * BT + bt
                sl = bt % 2
                w = pr.op("sync", "dma_start", XI[:, sl * 1024:(sl + 1) * 1024],
                          xres[s, t * P:(t + 1) * P, :],
                          waits=[xres_st[s * TPS + t]] + list(xi_free[sl]),
                          inc=cXs[sl], amt=16)
                xi_free[sl] = [(cXs[sl], w)]
                ap2 = pr.op("vector", "tensor_scalar",
                            LNT[:, sl * 1024:(sl + 1) * 1024],
                            XI[:, sl * 1024:(sl + 1) * 1024],
                            SVX[:, t:t + 1], SVX[:, TPS + t:TPS + t + 1],
                            ALU.subtract, ALU.mult,
                            waits=[(cXs[sl], w), svx_r[t]] + list(lnt_free[sl]),
                            inc=cV)
                xi_free[sl] = [(cV, ap2)]
                psl2 = PPt if sl == 0 else PPt2
                pi2 = 0 if sl == 0 else 1
                tp2 = None
                for dc_ in range(DC):
                    tp2 = pr.op("tensor", "transpose",
                                psl2[:, dc_ * P:(dc_ + 1) * P],
                                LNT[:, sl * 1024 + dc_ * P: sl * 1024 + (dc_ + 1) * P],
                                IDT[:],
                                waits=([(cV, ap2), (cC, CONSTW)] + list(ppt_free[pi2])) if dc_ == 0 else [],
                                inc=cT if dc_ == DC - 1 else None)
                lnt_free[sl] = [(cT, tp2)]
                info = (psl2, pi2, tp2, lnb[:, :, bt * P:(bt + 1) * P],
                        list(vtm_free) if bt == 0 else [])
                if bt == 0:
                    vtm_free = []
                if pend2 is not None:
                    evs2.append((bt - 1, ln_evac(pend2)))
                pend2 = info
            ev_last = ln_evac(pend2)
            evs2.append((BT - 1, ev_last))
            emit_gating2(evs2)
            evs2 = []
            g13_last = cV.n
            ab_free_in = list(ab_free)

            def emit_gates_tail():
                nonlocal gtb_free
                gtp = None
                for bt in range(BT):
                    gtp = pr.op("tensor", "transpose",
                                PPt2[0:8, bt * P:(bt + 1) * P],
                                GTB[:, bt * 8:(bt + 1) * 8], IDT[:],
                                waits=([(cV, g13_last)] + list(ppt_free[1])) if bt == 0 else [],
                                inc=cT if bt == BT - 1 else None)
                gtb_free = [(cT, gtp)]
                gfm_ = pr.op("scalar", "copy", AB[0:E, b * 1024:(b + 1) * 1024],
                             PPt2[0:E, :], waits=[(cT, gtp)] + ab_free_in,
                             inc=cS)
                ppt_free[1] = [(cS, gfm_)]
                return gfm_
            ln2_done = cS.n

            if dbg and s == 0 and b == 0:
                pr.op("gpsimd", "dma_start", dbg["dbg_lnb"][:], VTM[:, 0:8192],
                      waits=[(cS, ln2_done)], inc=cDB, amt=16)
                vtm_free = vtm_free + [(cDB, cDB.n)]

            ghA_done = 0
            ph_free = {0: pqf[0] + pqf[1], 1: pqf[2] + pqf[3]}
            ABH = AB[:, (1 - b) * 1024:(2 - b) * 1024]

            def w1mms(e, hc, first_w, e_rdy, w1t, w1base):
                mm = None
                for dc_ in range(DC):
                    for n2 in range(2):
                        first = dc_ == 0 and n2 == 0
                        mm = pr.op(
                            "tensor", "matmul", PQBK[(hc % 2) * 2 + n2][:, 0:512],
                            w1t[:, w1base + dc_ * 512 + hc * P: w1base + dc_ * 512 + (hc + 1) * P],
                            VTM[:, dc_ * 1024 + n2 * 512: dc_ * 1024 + (n2 + 1) * 512],
                            start=(dc_ == 0), stop=(dc_ == DC - 1),
                            waits=([e_rdy] + first_w) if first else [],
                            inc=cT if (dc_ == DC - 1 and n2 == 1) else None)
                return mm

            def e0_gelu(hc, mm, buf, extra):
                ge1_ = pr.op("scalar", "activation", buf[:, 0:512],
                             PQBK[(hc % 2) * 2][:, 0:512], AF.Gelu,
                             bias=B1SB[:, hc:hc + 1],
                             waits=[(cT, mm), (cC, CONSTW)] + extra, inc=cS)
                ge2_ = pr.op("scalar", "activation", buf[:, 512:1024],
                             PQBK[(hc % 2) * 2 + 1][:, 0:512], AF.Gelu,
                             bias=B1SB[:, hc:hc + 1], inc=cS)
                return ge2_

            e_rdy = w1pre.pop(0)
            w1t, w1base = w1slot(0)
            mm0 = w1mms(0, 0, [(cS, ev_last)] + ph_free[0], e_rdy, w1t, w1base)
            ge0 = e0_gelu(0, mm0, LNT[:, 0:1024], list(lnt_free[0]))
            mm1 = w1mms(0, 1, ph_free[1], e_rdy, w1t, w1base)
            ge1 = e0_gelu(1, mm1, ABH, list(ab_free))
            mm2 = w1mms(0, 2, [(cS, ge0)], e_rdy, w1t, w1base)
            mm3 = w1mms(0, 3, [(cS, ge1)], e_rdy, w1t, w1base)
            gfm = emit_gates_tail()
            bc = None
            for n2 in range(2):
                bc = pr.op("tensor", "matmul",
                           PPo[:, n2 * 512:(n2 + 1) * 512],
                           SEL8[:, 0:P],
                           AB[0:E, b * 1024 + n2 * 512: b * 1024 + (n2 + 1) * 512],
                           start=True, stop=True,
                           waits=([(cS, gfm), (cC, CONSTW)] + list(ppo_free)) if n2 == 0 else [],
                           inc=cT if n2 == 1 else None)
            bce = pr.op("scalar", "copy", LNT[:, 1024:2048], PPo[:, :],
                        waits=[(cT, bc)] + list(lnt_free[1]), inc=cS)
            ppo_free = [(cS, bce)]
            gh0 = pr.op("vector", "tensor_tensor", QFM[:, 0:1024],
                        LNT[:, 0:1024], LNT[:, 1024:2048], ALU.mult,
                        waits=[(cS, bce)] + list(qfm_free), inc=cV)
            qfm_free = []
            ge2 = e0_gelu(2, mm2, LNT[:, 0:1024], [(cV, gh0)])
            gh1 = pr.op("vector", "tensor_tensor", QFM[:, 1024:2048],
                        ABH, LNT[:, 1024:2048], ALU.mult, inc=cV)
            ge3 = e0_gelu(3, mm3, ABH, [(cV, gh1)])
            gh2 = pr.op("vector", "tensor_tensor", QFM[:, 2048:3072],
                        LNT[:, 0:1024], LNT[:, 1024:2048], ALU.mult,
                        waits=[(cS, ge2)], inc=cV)
            gh3 = pr.op("vector", "tensor_tensor", QFM[:, 3072:4096],
                        ABH, LNT[:, 1024:2048], ALU.mult,
                        waits=[(cS, ge3)], inc=cV)
            ph_free[0] = [(cS, ge2)]
            ph_free[1] = [(cS, ge3)]
            lnt_free[0] = [(cV, gh2)]
            lnt_free[1] = [(cV, gh3)]
            abh_free = [(cV, gh3)]
            w1slot_free[0] = [(cT, cT.n)]
            ghA_done = gh3

            for e in range(1, E):
                wsl = e % 3
                if e in w1pre:
                    e_rdy = w1pre.pop(e)
                else:
                    e_rdy = load_w1(e, pre=list(w1slot_free[wsl]))
                w1t, w1base = w1slot(e)
                bc = None
                for n2 in range(2):
                    bc = pr.op("tensor", "matmul",
                               PPo[:, n2 * 512:(n2 + 1) * 512],
                               SEL8[:, e * P:(e + 1) * P],
                               AB[0:E, b * 1024 + n2 * 512: b * 1024 + (n2 + 1) * 512],
                               start=True, stop=True,
                               waits=([(cS, gfm), (cC, CONSTW)] + list(ppo_free)) if n2 == 0 else [],
                               inc=cT if n2 == 1 else None)
                bce = pr.op("scalar", "copy", LNT[:, 1024:2048], PPo[:, :],
                            waits=[(cT, bc)] + list(lnt_free[1]), inc=cS)
                ppo_free = [(cS, bce)]
                gh_done = None
                for hc in range(HCN):
                    par = hc % 2
                    mm = None
                    for dc_ in range(DC):
                        for n2 in range(2):
                            first = dc_ == 0 and n2 == 0
                            mm = pr.op(
                                "tensor", "matmul", PQBK[par * 2 + n2][:, 0:512],
                                w1t[:, w1base + dc_ * 512 + hc * P: w1base + dc_ * 512 + (hc + 1) * P],
                                VTM[:, dc_ * 1024 + n2 * 512: dc_ * 1024 + (n2 + 1) * 512],
                                start=(dc_ == 0), stop=(dc_ == DC - 1),
                                waits=([e_rdy, (cS, ev_last)] + list(ph_free[par])) if first else [],
                                inc=cT if (dc_ == DC - 1 and n2 == 1) else None)
                    ge1 = pr.op("scalar", "activation", LNT[:, 0:512],
                                PQBK[par * 2][:, 0:512], AF.Gelu,
                                bias=B1SB[:, e * HCN + hc: e * HCN + hc + 1],
                                waits=[(cT, mm), (cC, CONSTW)] + list(lnt_free[0]),
                                inc=cS)
                    ge = pr.op("scalar", "activation", LNT[:, 512:1024],
                               PQBK[par * 2 + 1][:, 0:512], AF.Gelu,
                               bias=B1SB[:, e * HCN + hc: e * HCN + hc + 1],
                               inc=cS)
                    ph_free[par] = [(cS, ge)]
                    GHdst = QFM if e < 4 else KFM
                    gcol = ((e % 4) * 4 + hc) * 1024
                    pre = []
                    if e == 0 and hc == 0:
                        pre = list(qfm_free)
                        qfm_free = []
                    elif e == 4 and hc == 0:
                        pre = list(kfm_free)
                        kfm_free = []
                    gh_done = pr.op("vector", "tensor_tensor",
                                    GHdst[:, gcol:gcol + 1024], LNT[:, 0:1024],
                                    LNT[:, 1024:2048], ALU.mult,
                                    waits=[(cS, ge)] + pre, inc=cV)
                    lnt_free[0] = [(cV, gh_done)]
                w1slot_free[wsl] = [(cT, cT.n)]
                lnt_free[1] = [(cV, gh_done)]
                ghA_done = gh_done
            pqf[0] = list(ph_free[0])
            pqf[1] = list(ph_free[0])
            pqf[2] = list(ph_free[1])
            pqf[3] = list(ph_free[1])
            if b + 1 < NBLK:
                for e_ in range(3):
                    dst_, col_ = w1slot(e_)
                    w1pre[e_] = load_plain(
                        dst_, col_, 4096, dr["w1b"][e_],
                        pre=list(w1slot_free[e_ % 3]) + [pc["w1"][e_]],
                        eng="scalar")

            if dbg and s == 0 and b == 0:
                pr.op("gpsimd", "dma_start", dbg["dbg_gfm"][:, 0:1024],
                      AB[0:E, 0:1024], waits=[(cS, gfm), (cDB, cDB.n)],
                      inc=cDB, amt=16)
                pr.op("gpsimd", "dma_start", dbg["dbg_gh"][:], QFM[:, :],
                      waits=[(cV, ghA_done), (cDB, cDB.n)], inc=cDB, amt=16)
                ab_free = ab_free + [(cDB, cDB.n)]
                qfm_free = qfm_free + [(cDB, cDB.n)]

            for bt in range(BT):
                t = b * BT + bt
                sl = bt % 2
                w = pr.op("sync", "dma_start", XI[:, sl * 1024:(sl + 1) * 1024],
                          xres[s, t * P:(t + 1) * P, :],
                          waits=[xres_st[s * TPS + t]] + list(xi_free[sl]),
                          inc=cXs[sl], amt=16)
                if b == 0 and s + 1 < NSEQ:
                    prefetch_ln1_stats(s + 1, tiles=(2 * bt, 2 * bt + 1))
                use_ppo = bt % 2 == 0
                fr = ppo_free if use_ppo else (pqf[0] + pqf[1])
                mm = None
                for n2 in range(2):
                    po = PPo[:, n2 * 512:(n2 + 1) * 512] if use_ppo else PQBK[n2][:, 0:512]
                    for e in range(E):
                        GHsrc = QFM if e < 4 else KFM
                        w2src = LNFM if e < 4 else WBUF
                        for hc in range(HCN):
                            gcol = ((e % 4) * 4 + hc) * 1024 + bt * P
                            wcol = (e % 4) * 4096 + hc * 1024 + n2 * 512
                            first = n2 == 0 and e == 0 and hc == 0
                            ew = []
                            if first:
                                ew = [(cV, ghA_done)] + list(fr)
                            if bt == 0 and b == 0 and n2 == 0 and hc == 0:
                                ew = ew + [w2_rdy[e]]
                            mm = pr.op(
                                "tensor", "matmul", po,
                                GHsrc[:, gcol:gcol + P],
                                w2src[:, wcol:wcol + 512],
                                start=(e == 0 and hc == 0), stop=False,
                                waits=ew)
                    mm = pr.op("tensor", "matmul", po,
                               AB[0:E, b * 1024 + bt * P: b * 1024 + (bt + 1) * P],
                               B2SB[:, n2 * 512:(n2 + 1) * 512],
                               start=False, stop=True,
                               waits=[b2c] if n2 == 0 else [],
                               inc=cT if n2 == 1 else None)
                if use_ppo:
                    ad = pr.op("vector", "tensor_tensor",
                               XO[:, sl * 1024:(sl + 1) * 1024],
                               XI[:, sl * 1024:(sl + 1) * 1024], PPo[:, :],
                               ALU.add,
                               waits=[(cT, mm), (cXs[sl], w)] + list(xo_free[sl]),
                               inc=cV)
                    ppo_free = [(cV, ad)]
                else:
                    pr.op("vector", "tensor_tensor",
                          XO[:, sl * 1024:sl * 1024 + 512],
                          XI[:, sl * 1024:sl * 1024 + 512], PQ0a[:, 0:512],
                          ALU.add,
                          waits=[(cT, mm), (cXs[sl], w)] + list(xo_free[sl]),
                          inc=cV)
                    ad = pr.op("vector", "tensor_tensor",
                               XO[:, sl * 1024 + 512:(sl + 1) * 1024],
                               XI[:, sl * 1024 + 512:(sl + 1) * 1024],
                               PQ0b[:, 0:512], ALU.add, inc=cV)
                    pqf[0] = [(cV, ad)]
                    pqf[1] = [(cV, ad)]
                xi_free[sl] = [(cV, ad)]
                st = pr.op("scalar", "dma_start", yout[s, t * P:(t + 1) * P, :],
                           XO[:, sl * 1024:(sl + 1) * 1024], waits=[(cV, ad)],
                           inc=cSTs[sl], amt=16)
                xo_free[sl] = [(cSTs[sl], st)]
            ab_free = [(cT, cT.n)]
        lnfm_free = [(cT, cT.n)]
        vtm_free = [(cT, cT.n)]
        wslot_free = [[(cT, cT.n)], [(cT, cT.n)]]
        tbl_free = [[(cT, cT.n)], [(cT, cT.n)]]
        svx_free = [[(cV, cV.n)], [(cV, cV.n)]]
        svy_rd = [(cV, cV.n)]

    pr.wait("scalar", (cSTs[0], cSTs[0].n), (cSTs[1], cSTs[1].n))
    pr.wait("gpsimd", (cDB, cDB.n))


_NC_CACHE = {}


def _get_nc(debug=False):
    key = bool(debug)
    if key not in _NC_CACHE:
        _NC_CACHE[key] = build_nc(debug=debug)
    return _NC_CACHE[key]


def make_in_maps(L0_attn_x, L0_attn_wq, L0_attn_wk, L0_attn_wv, L0_attn_wo,
                 L0_ffn_gate_w, L0_ffn_gate_b, L0_ffn_w1, L0_ffn_b1,
                 L0_ffn_w2, L0_ffn_b2):
    f32 = np.float32
    x = np.ascontiguousarray(np.asarray(L0_attn_x, dtype=f32))
    wq = np.asarray(L0_attn_wq, dtype=f32)
    wk = np.asarray(L0_attn_wk, dtype=f32)
    wv = np.asarray(L0_attn_wv, dtype=f32)
    wo = np.asarray(L0_attn_wo, dtype=f32)
    gate_w = np.asarray(L0_ffn_gate_w, dtype=f32)
    gate_b = np.asarray(L0_ffn_gate_b, dtype=f32)
    w1 = np.asarray(L0_ffn_w1, dtype=f32)
    b1 = np.asarray(L0_ffn_b1, dtype=f32)
    w2 = np.asarray(L0_ffn_w2, dtype=f32)
    b2 = np.asarray(L0_ffn_b2, dtype=f32)

    perm = np.concatenate([np.arange(0, D, 2), np.arange(1, D, 2)])
    wqp = np.ascontiguousarray(wq.T[:, perm])
    wkp = np.ascontiguousarray(wk.T[:, perm])
    wvt = np.ascontiguousarray(wv.T)
    wot = np.ascontiguousarray(wo.T)

    import jax
    import jax.numpy as jnp
    with jax.default_device(jax.devices("cpu")[0]):
        inv = 1.0 / 10000.0 ** (jnp.arange(0, D, 2, dtype=jnp.float32) / D)
        tpos = jnp.arange(S, dtype=jnp.float32)
        ang = jnp.outer(tpos, inv)
        cosT = np.asarray(jnp.cos(ang))
        sinT = np.asarray(jnp.sin(ang))
    cosb = np.ascontiguousarray(cosT.T.reshape(4, P, S))
    sinb = np.ascontiguousarray(sinT.T.reshape(4, P, S))
    cosq = np.ascontiguousarray(cosb / 32.0)
    sinq = np.ascontiguousarray(sinb / 32.0)

    gwl = np.ascontiguousarray(
        gate_w.reshape(DC, P, E).transpose(1, 0, 2).reshape(P, DC * E))
    gbb = np.ascontiguousarray(np.broadcast_to(gate_b[None, :], (P, E)))
    b1l = np.ascontiguousarray(
        b1.reshape(E, HCN, P).transpose(2, 0, 1).reshape(P, E * HCN))
    ident = np.eye(P, dtype=ml_dtypes.bfloat16)
    sel8 = np.zeros((E, E * P), dtype=ml_dtypes.bfloat16)
    for e in range(E):
        sel8[e, e * P:(e + 1) * P] = 1.0

    shared = dict(wqp=wqp, wkp=wkp, wvt=wvt, wot=wot, cosb=cosb, sinb=sinb,
                  cosq=cosq, sinq=sinq,
                  gwl=gwl, gbb=gbb, w1l=np.ascontiguousarray(w1), b1l=b1l,
                  w2l=np.ascontiguousarray(w2), b2i=np.ascontiguousarray(b2),
                  idin=ident, sel8=sel8)
    in_maps = []
    for c in range(NCORE):
        m = dict(shared)
        m["xin"] = np.ascontiguousarray(x[c * NSEQ:(c + 1) * NSEQ])
        in_maps.append(m)
    return in_maps


def kernel(**inputs):
    in_maps = make_in_maps(**inputs)
    nc = _get_nc(debug=False)
    res = run_bass_kernel_spmd(nc, in_maps, core_ids=list(range(NCORE)))
    out = np.concatenate([r["yout"] for r in res.results], axis=0)
    return out.reshape(16, S, D)
